# version 09
# baseline (speedup 1.0000x reference)
import numpy as np
import ml_dtypes
from contextlib import ExitStack

import concourse.bass as bass
import concourse.mybir as mybir
from concourse.bass_utils import run_bass_kernel_spmd

F32 = mybir.dt.float32
BF16 = mybir.dt.bfloat16
AF = mybir.ActivationFunctionType
ALU = mybir.AluOpType

D = 2048
TOK = 2048
SFULL = 4096
TB = 512
NTB = TOK // TB
NDC = D // 128
IFF = 8192
NIC = IFF // 128
NH = 16
NG = 4
HD = 128
KVW = 512
EPS = 1e-5


class _Res:
    __slots__ = ("name", "writers", "readers")
    def __init__(self, name):
        self.name = name
        self.writers = []
        self.readers = []

class _Op:
    __slots__ = ("eng", "fn", "deps", "dma", "cc", "seq", "inc", "cum", "desc",
                 "sem_key", "sem_val", "issue_wait")
    def __init__(self, eng, fn, dma, cc, desc):
        self.eng = eng
        self.fn = fn
        self.deps = []
        self.dma = dma
        self.cc = cc
        self.seq = -1
        self.inc = dma or cc
        self.cum = 0
        self.desc = desc
        self.sem_key = None
        self.sem_val = 0
        self.issue_wait = None

class Sched:
    ENGS = ("T", "S", "V", "P", "Y")

    def __init__(self):
        self.ops = {e: [] for e in self.ENGS}
        self.order = []

    def res(self, name):
        return _Res(name)

    def emit(self, eng, fn, reads=(), writes=(), dma=False, cc=False, desc=""):
        op = _Op(eng, fn, dma, cc, desc)
        op.seq = len(self.ops[eng])
        deps = {}

        def add_dep(d):
            if d is None or d is op:
                return
            if d.eng == eng == "T" and not (d.dma or d.cc or op.dma or op.cc):
                return
            if d.dma or d.cc:
                deps[id(d)] = d
            else:
                k = "e" + d.eng
                cur = deps.get(k)
                if cur is None or d.seq > cur.seq:
                    deps[k] = d

        def add_reader(lst, d):
            if d.dma or d.cc:
                lst.append(d)
            else:
                for i, x in enumerate(lst):
                    if not (x.dma or x.cc) and x.eng == d.eng:
                        if d.seq > x.seq:
                            lst[i] = d
                        return
                lst.append(d)

        for r in reads:
            for d in r.writers:
                add_dep(d)
            add_reader(r.readers, op)
        for w in writes:
            if w.readers or not (dma and all(x.dma for x in w.writers)):
                for d in w.writers:
                    add_dep(d)
                for rd in w.readers:
                    add_dep(rd)
                w.writers = [op]
                w.readers = []
            else:
                w.writers.append(op)
        op.deps = list(deps.values())
        self.ops[eng].append(op)
        self.order.append(op)
        return op

    NPOOL = 8

    def finalize(self, nc, block):
        for op in self.order:
            for d in op.deps:
                d.inc = True
        sem_names = ["T", "S", "V"]
        for e in ("Y", "P"):
            sem_names += [f"{e}{i}" for i in range(self.NPOOL)]
        sem_names += ["C0", "C1"]
        es = ExitStack()
        sems = {n: es.enter_context(nc.semaphore(f"sem{n}")) for n in sem_names}
        self._es = es

        for e in ("T", "S", "V"):
            c = 0
            for op in self.ops[e]:
                if op.inc:
                    c += 1
                op.cum = c
                op.sem_key = e
                op.sem_val = c
        for e in ("Y", "P"):
            ndma = 0
            ncc = 0
            uses = {}
            for op in self.ops[e]:
                if op.dma:
                    key = f"{e}{ndma % self.NPOOL}"
                    ndma += 1
                elif op.cc:
                    key = f"C{ncc % 2}"
                    ncc += 1
                else:
                    raise AssertionError(f"non-dma non-cc op on queue {e}: {op.desc}")
                u = uses.get(key, 0)
                uses[key] = u + 1
                step = 16 if op.dma else 1
                op.sem_key = key
                op.sem_val = (u + 1) * step
                if u > 0:
                    op.issue_wait = (key, u * step)

        def mk_stream(ename):
            def stream(eng):
                waited = {}
                def wait(key, v):
                    if waited.get(key, 0) < v:
                        eng.wait_ge(sems[key], v)
                        waited[key] = v
                for op in self.ops[ename]:
                    if op.issue_wait is not None:
                        wait(*op.issue_wait)
                    byk = {}
                    for d in op.deps:
                        byk[d.sem_key] = max(byk.get(d.sem_key, 0), d.sem_val)
                    for k, v in byk.items():
                        wait(k, v)
                    inst = op.fn(eng)
                    if op.inc:
                        inst.then_inc(sems[op.sem_key], 16 if op.dma else 1)
                if ename == "Y":
                    fin = {}
                    for op in self.ops["Y"]:
                        if op.dma:
                            fin[op.sem_key] = max(fin.get(op.sem_key, 0), op.sem_val)
                    for k, v in fin.items():
                        wait(k, v)
            return stream

        block.tensor(mk_stream("T"))
        block.scalar(mk_stream("S"))
        block.vector(mk_stream("V"))
        block.gpsimd(mk_stream("P"))
        block.sync(mk_stream("Y"))

    def force_inc_last(self):
        for e in self.ENGS:
            if self.ops[e]:
                self.ops[e][-1].inc = True



def _pt_lhsT(w):
    Dd, O = w.shape
    return np.ascontiguousarray(
        w.reshape(Dd // 128, 128, O // 128, 128).transpose(2, 1, 0, 3)
         .reshape(O // 128, 128, (Dd // 128) * 128))

def _perm_cols(w):
    Dd, I = w.shape
    return np.ascontiguousarray(w.reshape(Dd, I // 4, 4).transpose(0, 2, 1).reshape(Dd, I))

def _perm_rows(w):
    I, Dd = w.shape
    return np.ascontiguousarray(w.reshape(I // 4, 4, Dd).transpose(1, 0, 2).reshape(I, Dd))



def build_program(last_phase="ffn2"):
    nc = bass.Bass("TRN2", target_bir_lowering=False)
    sch = Sched()
    PH = ["ffn1", "qkv", "gather", "attn", "attnres", "ffn2"]
    phases = [] if last_phase == "ln0" else PH[: PH.index(last_phase) + 1]

    xin = nc.dram_tensor("xin", [TOK, D], F32, kind="ExternalInput").ap()
    wgu = nc.dram_tensor("wgu", [NIC, 128, 2 * D], F32, kind="ExternalInput").ap()
    wdn = nc.dram_tensor("wdn", [NDC, 128, IFF], F32, kind="ExternalInput").ap()
    wqk = nc.dram_tensor("wqk", [20, 128, D], F32, kind="ExternalInput").ap()
    wv = nc.dram_tensor("wv", [NDC, 128, KVW], F32, kind="ExternalInput").ap()
    wo = nc.dram_tensor("wo", [NDC, 128, D], F32, kind="ExternalInput").ap()
    bqk_in = nc.dram_tensor("bqk", [128, 20], F32, kind="ExternalInput").ap()
    bv_in = nc.dram_tensor("bv", [1, KVW], F32, kind="ExternalInput").ap()
    bo_in = nc.dram_tensor("bo", [128, 16], F32, kind="ExternalInput").ap()
    ones_in = nc.dram_tensor("ones_in", [128, 128], F32, kind="ExternalInput").ap()
    id_in = nc.dram_tensor("id_in", [128, 128], F32, kind="ExternalInput").ap()
    outx = nc.dram_tensor("outx", [TOK, D], F32, kind="ExternalOutput").ap()

    x2d = nc.dram_tensor("x2d", [TOK, D], F32).ap()
    x3d = nc.dram_tensor("x3d", [TOK, D], F32).ap()
    aod = nc.dram_tensor("aod", [D, TOK], BF16).ap()
    wguc = nc.dram_tensor("wguc", [NIC, 128, 2 * D], BF16).ap()
    wdnc = nc.dram_tensor("wdnc", [NDC, 128, IFF], BF16).ap()
    wqkc = nc.dram_tensor("wqkc", [20, 128, D], BF16).ap()
    woc = nc.dram_tensor("woc", [NDC, 128, D], BF16).ap()
    ksend_t = nc.dram_tensor("ksend", [NTB, KVW, TB], BF16)
    krecv_t = nc.dram_tensor("krecv", [NTB, 2 * KVW, TB], BF16)
    vsend_t = nc.dram_tensor("vsend", [NTB, TB, KVW], BF16)
    vrecv_t = nc.dram_tensor("vrecv", [NTB, 2 * TB, KVW], BF16)
    ksend, krecv = ksend_t.ap(), krecv_t.ap()
    vsend, vrecv = vsend_t.ap(), vrecv_t.ap()

    es = ExitStack()
    sb = lambda name, shape, dt: es.enter_context(nc.sbuf_tensor(name, shape, dt))
    ps = lambda name, shape, dt: es.enter_context(nc.psum_tensor(name, shape, dt))

    xt = [sb(f"xt{i}", [128, D], F32) for i in range(2)]
    z = sb("z", [128, NDC * TB], BF16)
    hsq = sb("hsq", [128, 64 * 512], BF16)
    warn = sb("warn", [128, 64 * 512 // 2 * 2], BF16)
    hq = sb("hq", [128, 4 * TB], F32)
    habs = sb("habs", [128, 4 * TB], F32)
    norm = sb("norm", [128, D], BF16)
    woutb = [sb(f"woutb{i}", [128, D], BF16) for i in range(2)]
    aout = sb("aout", [128, TB], BF16)
    pt = sb("ptile", [128, 8 * 512], BF16)
    rcp = sb("rcp", [128, 512], F32)
    spt = sb("spt", [128, 3 * TB], F32)
    stats = sb("stats", [128, 16], F32)
    bqk_sb = sb("bqk_sb", [128, 20], F32)
    bo_sb = sb("bo_sb", [128, 16], F32)
    bv_sb = sb("bv_sb", [1, KVW], BF16)
    ones = sb("ones", [128, 128], BF16)
    ident = sb("ident", [128, 128], BF16)

    pApair = ps("pA", [128, 1024], F32)
    pBpair = ps("pB", [128, 1024], F32)
    pC = ps("pC", [128, 512], F32)
    pD = ps("pD", [128, 512], F32)
    pf = [pC[:, :], pD[:, :], pApair[:, 0:512], pBpair[:, 0:512],
          pApair[:, 512:1024], pBpair[:, 512:1024]]
    pb = [ps("pb0", [128, 1024], BF16)[:, :], ps("pb1", [128, 1024], BF16)[:, :]]

    R = sch.res
    r_xt = [R(f"xt{i}") for i in range(2)]
    r_z = R("z")
    r_hs = [R(f"hs{i}") for i in range(64)]
    r_wz = [R(f"wz{i}") for i in range(8)]
    r_hq = R("hq")
    r_habs = R("habs")
    r_norm = R("norm")
    r_woutb = [R(f"woutb{i}") for i in range(2)]
    r_aout = R("aout")
    r_pt = [R(f"pt{i}") for i in range(3)]
    r_ptq = [R(f"ptq{i}") for i in range(2)]
    r_rcp = R("rcp")
    r_spt = R("spt")
    r_stats2 = [R("statsA"), R("statsB")]
    r_pf = [R(f"pf{i}") for i in range(6)]
    r_pb = [R(f"pb{i}") for i in range(2)]
    r_x2dT = [R(f"x2d{i}") for i in range(16)]
    r_x3dT = [R(f"x3d{i}") for i in range(16)]
    r_outT = [R(f"out{i}") for i in range(16)]
    r_xinT = [R("xin")] * 16
    r_aodQ = [R(f"aod{i}") for i in range(NTB)]
    r_ksend = [R(f"ksend{i}") for i in range(NTB)]
    r_krecv = [R(f"krecv{i}") for i in range(NTB)]
    r_vsend = [R(f"vsend{i}") for i in range(NTB)]
    r_vrecv = [R(f"vrecv{i}") for i in range(NTB)]
    r_const = R("const")
    r_wguc = R("wguc")
    r_wdnc = R("wdnc")
    r_wqkc = R("wqkc")
    r_woc = R("woc")

    E = sch.emit

    E("Y", lambda e: e.dma_start(out=bqk_sb[:, :], in_=bqk_in), writes=[r_const], dma=True, desc="bqk")
    E("Y", lambda e: e.dma_start(out=bo_sb[:, :], in_=bo_in), writes=[r_const], dma=True, desc="bo")
    E("P", lambda e: e.dma_start(out=bv_sb[:, :], in_=bv_in), writes=[r_const], dma=True, desc="bv")
    E("P", lambda e: e.dma_start(out=ones[:, :], in_=ones_in), writes=[r_const], dma=True, desc="ones")
    E("P", lambda e: e.dma_start(out=ident[:, :], in_=id_in), writes=[r_const], dma=True, desc="ident")

    z3 = z[:, :].rearrange("p (c n) -> p c n", n=TB)
    hs3 = hsq[:, :].rearrange("p (c n) -> p c n", n=512)
    w_all = warn[:, :]

    def wzone(i0, n, shape_cols):
        apv = warn[:, i0 * 4096:(i0 + n) * 4096]
        return apv

    def emit_ln(src_ap, r_srcT, b, lnname, res_in=None):
        for t in range(4):
            sl = t % 2
            gt = b * 4 + t
            if res_in is None:
                E("Y", lambda e, sl=sl, gt=gt: e.dma_start(
                    out=xt[sl][:, :], in_=src_ap[gt * 128:(gt + 1) * 128, :]),
                    reads=[r_srcT[gt]], writes=[r_xt[sl]], dma=True, desc=f"{lnname} ldx{gt}")
            else:
                base_ap, r_baseT, add_fm, r_addQ, store_ap, r_storeT = res_in
                E("Y", lambda e, sl=sl, gt=gt, base_ap=base_ap: e.dma_start(
                    out=xt[sl][:, :], in_=base_ap[gt * 128:(gt + 1) * 128, :]),
                    reads=[r_baseT[gt]], writes=[r_xt[sl]], dma=True, desc=f"{lnname} ldb{gt}")
                E("Y", lambda e, gt=gt, add_fm=add_fm: e.dma_start_transpose(
                    norm[:, :], add_fm[:, gt * 128:(gt + 1) * 128]),
                    reads=[r_addQ[gt // 4]], writes=[r_norm], dma=True, desc=f"{lnname} atr{gt}")
                E("V", lambda e, sl=sl: e.tensor_tensor(
                    xt[sl][:, :], xt[sl][:, :], norm[:, :], ALU.add),
                    reads=[r_norm, r_xt[sl]], writes=[r_xt[sl]], desc=f"{lnname} ares{gt}")
                E("Y", lambda e, sl=sl, gt=gt, store_ap=store_ap: e.dma_start(
                    out=store_ap[gt * 128:(gt + 1) * 128, :], in_=xt[sl][:, :]),
                    reads=[r_xt[sl]], writes=[r_storeT[gt]], dma=True, desc=f"{lnname} stx{gt}")
            c0 = (t % 2) * 4
            r_stats = r_stats2[t % 2]
            E("S", lambda e, sl=sl, c0=c0: e.activation(
                norm[:, :], xt[sl][:, :], AF.Identity, accum_out=stats[:, c0:c0 + 1]),
                reads=[r_xt[sl]], writes=[r_norm, r_stats], desc=f"{lnname} sum{t}")
            E("S", lambda e, sl=sl, c0=c0: e.activation(
                norm[:, :], xt[sl][:, :], AF.Square, accum_out=stats[:, c0 + 1:c0 + 2]),
                reads=[r_xt[sl]], writes=[r_norm, r_stats], desc=f"{lnname} sq{t}")
            E("V", lambda e, c0=c0: e.tensor_scalar(
                stats[:, c0 + 2:c0 + 3], stats[:, c0:c0 + 1], -1.0 / D, None, ALU.mult),
                reads=[r_stats], writes=[r_stats], desc="negmean")
            E("V", lambda e, c0=c0: e.tensor_scalar(
                stats[:, c0 + 3:c0 + 4], stats[:, c0 + 1:c0 + 2], 1.0 / D, EPS, ALU.mult, ALU.add),
                reads=[r_stats], writes=[r_stats], desc="ex2e")
            E("V", lambda e, c0=c0: e.tensor_tensor(
                stats[:, c0:c0 + 1], stats[:, c0 + 2:c0 + 3], stats[:, c0 + 2:c0 + 3], ALU.mult),
                reads=[r_stats], writes=[r_stats], desc="msq")
            E("V", lambda e, c0=c0: e.tensor_tensor(
                stats[:, c0 + 1:c0 + 2], stats[:, c0 + 3:c0 + 4], stats[:, c0:c0 + 1], ALU.subtract),
                reads=[r_stats], writes=[r_stats], desc="vpe")
            E("S", lambda e, c0=c0: e.activation(
                stats[:, c0:c0 + 1], stats[:, c0 + 1:c0 + 2], AF.Ln),
                reads=[r_stats], writes=[r_stats], desc="ln")
            E("S", lambda e, c0=c0: e.activation(
                stats[:, c0 + 1:c0 + 2], stats[:, c0:c0 + 1], AF.Exp, scale=-0.5),
                reads=[r_stats], writes=[r_stats], desc="rstd")
            E("V", lambda e, sl=sl, c0=c0: e.tensor_scalar(
                norm[:, :], xt[sl][:, :], stats[:, c0 + 2:c0 + 3], stats[:, c0 + 1:c0 + 2],
                ALU.add, ALU.mult),
                reads=[r_xt[sl], r_stats], writes=[r_norm], desc=f"{lnname} nrm{t}")
            for h in range(2):
                pbi = 1
                for j in range(8):
                    dc = h * 8 + j
                    E("T", lambda e, pbi=pbi, j=j, dc=dc, last=(j == 7): e.transpose(
                        pb[pbi][:, j * 128:(j + 1) * 128], norm[:, dc * 128:(dc + 1) * 128],
                        ident[:, :]),
                        reads=[r_norm, r_const], writes=[r_pb[pbi]], desc=f"tr{dc}")
                E("V", lambda e, pbi=pbi, h=h, t=t: e.tensor_copy(
                    z3[:, h * 8:(h + 1) * 8, t * 128:(t + 1) * 128],
                    pb[pbi][:, :].rearrange("p (c n) -> p c n", n=128)),
                    reads=[r_pb[pbi]], writes=[r_z], desc=f"zcp{t}.{h}")

    def emit_ffn(src_ap, r_srcT, dst_ap, r_dstT, tag, res_in=None):
        def emit_gu_load(c, cold):
            wsl = c % 3
            if cold:
                E("P", lambda e, c=c, wsl=wsl: e.dma_start(
                    out=warn[:, wsl * 4096:(wsl + 1) * 4096], in_=wgu[c]),
                    writes=[r_wz[wsl]], dma=True, desc=f"{tag}wgu{c}")
                E("Y", lambda e, c=c, wsl=wsl: e.dma_start(
                    out=wguc[c], in_=warn[:, wsl * 4096:(wsl + 1) * 4096]),
                    reads=[r_wz[wsl]], writes=[r_wguc], dma=True, desc=f"cgu{c}")
            else:
                E("P", lambda e, c=c, wsl=wsl: e.dma_start(
                    out=warn[:, wsl * 4096:(wsl + 1) * 4096], in_=wguc[c]),
                    reads=[r_wguc], writes=[r_wz[wsl]], dma=True, desc=f"{tag}bgu{c}")

        chunk_of = [k * 16 + q for q in range(16) for k in range(4)]
        for b in range(NTB):
            emit_ln(src_ap, r_srcT, b, f"{tag}ln{b}", res_in=res_in)
            E("P", lambda e, b=b: e.dma_start(
                out=dst_ap[b * TB:(b + 1) * TB, :], in_=src_ap[b * TB:(b + 1) * TB, :]),
                reads=[r_srcT[i] for i in range(b * 4, b * 4 + 4)],
                writes=[r_dstT[i] for i in range(b * 4, b * 4 + 4)],
                dma=True, desc=f"{tag}pre{b}")
            cold = (tag == "f1" and b == 0)
            next_load = 0 if b == 0 else 3
            for q in range(16):
                for k in range(4):
                    c = k * 16 + q
                    i = q * 4 + k
                    while next_load <= min(i + 2, 63):
                        emit_gu_load(chunk_of[next_load], cold)
                        next_load += 1
                    wsl = c % 3
                    gb = [0, 1][(q * 4 + k) % 2]
                    ub = [2, 3][(q * 4 + k) % 2]
                    pg, pu = pf[gb], pf[ub]
                    rg, ru = r_pf[gb], r_pf[ub]
                    for dc in range(NDC):
                        E("T", lambda e, pg=pg, wsl=wsl, dc=dc, first=(dc == 0), last=(dc == 15):
                            e.matmul(pg[:, :], warn[:, wsl * 4096 + dc * 128: wsl * 4096 + (dc + 1) * 128],
                                     z3[:, dc, :], start=first, stop=last),
                            reads=[r_wz[wsl], r_z], writes=[rg], desc=f"g{c}.{dc}")
                    for dc in range(NDC):
                        E("T", lambda e, pu=pu, wsl=wsl, dc=dc, first=(dc == 0), last=(dc == 15):
                            e.matmul(pu[:, :], warn[:, wsl * 4096 + 2048 + dc * 128: wsl * 4096 + 2048 + (dc + 1) * 128],
                                     z3[:, dc, :], start=first, stop=last),
                            reads=[r_wz[wsl], r_z], writes=[ru], desc=f"u{c}.{dc}")
                    E("S", lambda e, pg=pg, k=k: e.activation(
                        hq[:, k * TB:(k + 1) * TB], pg[:, :], AF.Silu),
                        reads=[rg], writes=[r_hq], desc=f"silu{c}")
                    E("V", lambda e, pu=pu, k=k: e.tensor_tensor(
                        hq[:, k * TB:(k + 1) * TB], hq[:, k * TB:(k + 1) * TB], pu[:, :], ALU.mult),
                        reads=[ru, r_hq], writes=[r_hq], desc=f"hmul{c}")
                E("S", lambda e: e.activation(habs[:, :], hq[:, :], AF.Abs),
                    reads=[r_hq], writes=[r_habs], desc=f"abs{q}")
                A = lambda k: habs[:, k * TB:(k + 1) * TB]
                t0, t1, t2 = spt[:, 0:TB], spt[:, TB:2 * TB], spt[:, 2 * TB:3 * TB]
                seq = [
                    (t0, A(0), A(1), ALU.max), (t1, A(2), A(3), ALU.max),
                    (t0, t0, t1, ALU.min),
                    (t1, A(0), A(1), ALU.min), (t2, A(2), A(3), ALU.min),
                    (t1, t1, t2, ALU.max),
                    (t1, t0, t1, ALU.max),
                ]
                for (o, i0, i1, opk) in seq:
                    E("V", lambda e, o=o, i0=i0, i1=i1, opk=opk: e.tensor_tensor(o, i0, i1, opk),
                        reads=[r_habs, r_spt], writes=[r_spt], desc="m2")
                for k in range(4):
                    c = k * 16 + q
                    E("V", lambda e, k=k: e.tensor_tensor(
                        A(k), A(k), spt[:, TB:2 * TB], ALU.is_ge),
                        reads=[r_habs, r_spt], writes=[r_habs], desc=f"ge{c}")
                    E("V", lambda e, k=k, c=c: e.tensor_tensor(
                        hs3[:, c, :], hq[:, k * TB:(k + 1) * TB], A(k), ALU.mult),
                        reads=[r_habs, r_hq], writes=[r_hs[c]], desc=f"hs{c}")
            for oc in range(NDC):
                dsl = 3 + (oc % 2) * 2
                if cold:
                    E("P", lambda e, oc=oc, dsl=dsl: e.dma_start(
                        out=warn[:, dsl * 4096:(dsl + 2) * 4096], in_=wdn[oc]),
                        writes=[r_wz[dsl], r_wz[dsl + 1]], dma=True, desc=f"{tag}wdn{oc}")
                    E("Y", lambda e, oc=oc, dsl=dsl: e.dma_start(
                        out=wdnc[oc], in_=warn[:, dsl * 4096:(dsl + 2) * 4096]),
                        reads=[r_wz[dsl], r_wz[dsl + 1]], writes=[r_wdnc], dma=True,
                        desc=f"cdn{oc}")
                else:
                    E("P", lambda e, oc=oc, dsl=dsl: e.dma_start(
                        out=warn[:, dsl * 4096:(dsl + 2) * 4096], in_=wdnc[oc]),
                        reads=[r_wdnc], writes=[r_wz[dsl], r_wz[dsl + 1]], dma=True,
                        desc=f"{tag}bdn{oc}")
                pd = pf[[0, 1, 2][oc % 3]]
                rd = r_pf[[0, 1, 2][oc % 3]]
                ics = [q2 + 16 * k2 for q2 in range(16) for k2 in range(4)]
                for idx, ic in enumerate(ics):
                    E("T", lambda e, pd=pd, dsl=dsl, ic=ic, first=(idx == 0), last=(idx == 63):
                        e.matmul(pd[:, :], warn[:, dsl * 4096 + ic * 128: dsl * 4096 + (ic + 1) * 128],
                                 hs3[:, ic, :], start=first, stop=last),
                        reads=[r_wz[dsl], r_wz[dsl + 1], r_hs[ic]], writes=[rd], desc=f"d{oc}.{ic}")
                E("S", lambda e, pd=pd: e.mul(aout[:, :], pd[:, :], 0.5),
                    reads=[rd], writes=[r_aout], desc=f"dev{oc}")
                pbi = 0
                for tt in range(4):
                    E("T", lambda e, pbi=pbi, tt=tt: e.transpose(
                        pb[pbi][:, tt * 128:(tt + 1) * 128],
                        aout[:, tt * 128:(tt + 1) * 128], ident[:, :]),
                        reads=[r_aout, r_const], writes=[r_pb[pbi]], desc=f"dtr{oc}.{tt}")
                E("V", lambda e, pbi=pbi, oc=oc: e.tensor_copy(
                    z[:, :].rearrange("p (t n) -> p t n", n=D)[:, 0:4, oc * 128:(oc + 1) * 128],
                    pb[pbi][:, 0:512].rearrange("p (t n) -> p t n", n=128)),
                    reads=[r_pb[pbi]], writes=[r_z], desc=f"dcp{oc}")
            if b + 1 < NTB:
                ncold = (tag == "f1" and b + 1 == 0)
                for j in range(3):
                    emit_gu_load(chunk_of[j], ncold)
            ztok = z[:, :].rearrange("p (t n) -> p t n", n=D)
            for tt in range(4):
                gt = b * 4 + tt
                E("P", lambda e, tt=tt, gt=gt: e.dma_start(
                    out=dst_ap[gt * 128:(gt + 1) * 128, :], in_=ztok[:, tt, :],
                    accum_op=ALU.add),
                    reads=[r_z, r_dstT[gt]], writes=[r_dstT[gt]], dma=True,
                    desc=f"{tag}acc{gt}")

    def emit_qkv():
        for dc in range(NDC):
            E("P", lambda e, dc=dc: e.dma_start(
                out=warn[:, dc * 512:(dc + 1) * 512], in_=wv[dc]),
                writes=[r_wz[dc // 8]], dma=True, desc=f"wv{dc}")
        wv3 = warn[:, 0:NDC * 512].rearrange("p (c n) -> p c n", n=512)
        for b in range(NTB):
            emit_ln(x2d, r_x2dT, b, f"qln{b}")
            for oc in range(20):
                wsl = 4 + (oc % 2)
                if b == 0:
                    E("P", lambda e, oc=oc, wsl=wsl: e.dma_start(
                        out=warn[:, wsl * 4096: wsl * 4096 + 2048], in_=wqk[oc]),
                        writes=[r_wz[wsl]], dma=True, desc=f"wqk{oc}")
                    E("Y", lambda e, oc=oc, wsl=wsl: e.dma_start(
                        out=wqkc[oc], in_=warn[:, wsl * 4096: wsl * 4096 + 2048]),
                        reads=[r_wz[wsl]], writes=[r_wqkc], dma=True, desc=f"cqk{oc}")
                else:
                    E("P", lambda e, oc=oc, wsl=wsl: e.dma_start(
                        out=warn[:, wsl * 4096: wsl * 4096 + 2048], in_=wqkc[oc]),
                        reads=[r_wqkc], writes=[r_wz[wsl]], dma=True, desc=f"bqk{oc}")
                pq = pf[oc % 4]
                rq = r_pf[oc % 4]
                for dc in range(NDC):
                    E("T", lambda e, pq=pq, wsl=wsl, dc=dc, first=(dc == 0), last=(dc == 15):
                        e.matmul(pq[:, :], warn[:, wsl * 4096 + dc * 128: wsl * 4096 + (dc + 1) * 128],
                                 z3[:, dc, :], start=first, stop=last),
                        reads=[r_wz[wsl], r_z], writes=[rq], desc=f"qk{oc}.{dc}"),
                if oc < 16:
                    E("S", lambda e, pq=pq, oc=oc, b=b: e.activation(
                        hsq[:, oc * 2048 + b * TB: oc * 2048 + (b + 1) * TB], pq[:, :],
                        AF.Identity, bias=bqk_sb[:, oc:oc + 1]),
                        reads=[rq, r_const], writes=[r_hs[oc]], desc=f"qev{oc}"),
                else:
                    g = oc - 16
                    psl = (oc + b) % 2
                    E("S", lambda e, pq=pq, oc=oc, psl=psl: e.activation(
                        pt[:, 3072 + psl * 512: 3072 + (psl + 1) * 512], pq[:, :],
                        AF.Identity, bias=bqk_sb[:, oc:oc + 1]),
                        reads=[rq, r_const], writes=[r_ptq[psl]], desc=f"kev{oc}"),
                    E("Y", lambda e, g=g, b=b, psl=psl: e.dma_start(
                        out=ksend[b, g * 128:(g + 1) * 128, :],
                        in_=pt[:, 3072 + psl * 512: 3072 + (psl + 1) * 512]),
                        reads=[r_ptq[psl]], writes=[r_ksend[b]], dma=True, desc=f"kst{g}.{b}")
            for tt in range(4):
                pv = pf[4 + tt % 2]
                rv = r_pf[4 + tt % 2]
                for dc in range(NDC):
                    E("T", lambda e, pv=pv, dc=dc, tt=tt, first=(dc == 0):
                        e.matmul(pv[:, :], z3[:, dc, tt * 128:(tt + 1) * 128],
                                 wv3[:, dc, :], start=first, stop=False),
                        reads=[r_wz[0], r_wz[1], r_z], writes=[rv], desc=f"v{tt}.{dc}")
                E("T", lambda e, pv=pv: e.matmul(
                    pv[:, :], ones[0:1, :], bv_sb[:, :], start=False, stop=True),
                    reads=[r_const], writes=[rv], desc=f"vb{tt}")
                psl = tt % 2
                E("S", lambda e, pv=pv, psl=psl: e.activation(
                    pt[:, 3072 + psl * 512: 3072 + (psl + 1) * 512], pv[:, :], AF.Identity),
                    reads=[rv], writes=[r_ptq[psl]], desc=f"vev{tt}")
                E("Y", lambda e, b=b, tt=tt, psl=psl: e.dma_start(
                    out=vsend[b, tt * 128:(tt + 1) * 128, :],
                    in_=pt[:, 3072 + psl * 512: 3072 + (psl + 1) * 512]),
                    reads=[r_ptq[psl]], writes=[r_vsend[b]], dma=True, desc=f"vst{b}.{tt}")
            emit_gather_block(b)

    GROUPS = [[0, 1], [2, 3], [4, 5], [6, 7]]

    def emit_gather_block(b):
        E("P", lambda e, b=b: e.collective_compute(
            "AllGather", ALU.bypass, replica_groups=GROUPS,
            ins=[ksend_t.ap()[b].opt()], outs=[krecv_t.ap()[b].opt()]),
            reads=[r_ksend[b]], writes=[r_krecv[b]], cc=True, desc=f"ccK{b}")
        E("P", lambda e, b=b: e.collective_compute(
            "AllGather", ALU.bypass, replica_groups=GROUPS,
            ins=[vsend_t.ap()[b].opt()], outs=[vrecv_t.ap()[b].opt()]),
            reads=[r_vsend[b]], writes=[r_vrecv[b]], cc=True, desc=f"ccV{b}")

    def emit_gather(b):
        for g in range(NG):
            for r in range(2):
                E("Y", lambda e, g=g, r=r, b=b: e.dma_start(
                    out=warn[:, g * 4096 + r * 2048 + b * TB: g * 4096 + r * 2048 + (b + 1) * TB],
                    in_=krecv[b, r * KVW + g * 128: r * KVW + (g + 1) * 128, :]),
                    reads=[r_krecv[b]], writes=[r_wz[g]], dma=True, desc=f"kld{g}.{r}.{b}")
        for r in range(2):
            base = 4 * 4096 + (r * 16 + b * 4) * 512
            E("Y", lambda e, r=r, b=b, base=base: e.dma_start(
                out=warn[:, base: base + 4 * 512].rearrange("p (t n) -> p t n", n=512),
                in_=vrecv[b, r * TB:(r + 1) * TB, :].rearrange("(t p) n -> p t n", p=128)),
                reads=[r_vrecv[b]], writes=[r_wz[4 + (r * 16 + b * 4) // 8]],
                dma=True, desc=f"vld{r}.{b}")

    def emit_attn():
        k3 = warn[:, 0:4 * 4096].rearrange("p (g n) -> p g n", n=4096)
        v3 = warn[:, 4 * 4096:8 * 4096].rearrange("p (t n) -> p t n", n=512)
        q3 = hsq[:, :].rearrange("p (h n) -> p h n", n=2048)
        NKT = SFULL // 128
        for qb in range(NTB):
            for h in range(NH):
                g = h % NG
                po = pf[0]
                pr = pf[1]
                ro, rr = r_pf[0], r_pf[1]
                ST_BANKS = [2, 4, 3, 5]
                def st_op(kt):
                    bank = ST_BANKS[kt % 4]
                    E("T", lambda e, kt=kt, bank=bank, g=g, h=h, qb=qb: e.matmul(
                        pf[bank][:, :], k3[:, g, kt * 128:(kt + 1) * 128],
                        q3[:, h, qb * TB:(qb + 1) * TB], start=True, stop=True),
                        reads=[r_wz[g], r_hs[h]], writes=[r_pf[bank]], desc=f"st{h}.{kt}")
                for kt in range(4):
                    st_op(kt)
                for kp in range(NKT // 2):
                    k0 = 2 * kp
                    pair_t = pApair if (k0 % 4) == 0 else pBpair
                    prs = [r_pf[2], r_pf[4]] if (k0 % 4) == 0 else [r_pf[3], r_pf[5]]
                    psl = kp % 3
                    E("S", lambda e, pair_t=pair_t, psl=psl: e.activation(
                        pt[:, psl * 1024:(psl + 1) * 1024],
                        pair_t[:, :], AF.Exp, scale=1.0 / HD),
                        reads=prs, writes=[r_pt[psl]], desc=f"exp{h}.{kp}")
                    if k0 + 5 < NKT:
                        st_op(k0 + 4)
                        st_op(k0 + 5)
                    for j in range(2):
                        kt = k0 + j
                        pslc = pt[:, psl * 1024 + j * 512: psl * 1024 + (j + 1) * 512]
                        E("T", lambda e, kt=kt, g=g, pslc=pslc, first=(kt == 0), last=(kt == NKT - 1):
                            e.matmul(po[:, :], v3[:, kt, g * 128:(g + 1) * 128],
                                     pslc, start=first, stop=last),
                            reads=[r_wz[4 + kt // 8], r_pt[psl]], writes=[ro], desc=f"pv{h}.{kt}")
                        E("T", lambda e, kt=kt, pslc=pslc, first=(kt == 0), last=(kt == NKT - 1):
                            e.matmul(pr[:, :], ones[:, :], pslc, start=first, stop=last),
                            reads=[r_const, r_pt[psl]], writes=[rr], desc=f"rs{h}.{kt}")
                E("V", lambda e, pr=pr: e.reciprocal(rcp[:, :], pr[:, :]),
                    reads=[rr], writes=[r_rcp], desc=f"rcp{h}")
                E("V", lambda e, po=po, h=h: e.tensor_tensor(
                    z3[:, h, :], po[:, :], rcp[:, :], ALU.mult),
                    reads=[ro, r_rcp], writes=[r_z], desc=f"at{h}")
            for oc in range(NDC):
                wsl = oc % 2
                if qb == 0:
                    E("P", lambda e, oc=oc, wsl=wsl: e.dma_start(
                        out=woutb[wsl][:, :], in_=wo[oc]),
                        writes=[r_woutb[wsl]], dma=True, desc=f"wo{oc}")
                    E("Y", lambda e, oc=oc, wsl=wsl: e.dma_start(
                        out=woc[oc], in_=woutb[wsl][:, :]),
                        reads=[r_woutb[wsl]], writes=[r_woc], dma=True, desc=f"cwo{oc}")
                else:
                    E("P", lambda e, oc=oc, wsl=wsl: e.dma_start(
                        out=woutb[wsl][:, :], in_=woc[oc]),
                        reads=[r_woc], writes=[r_woutb[wsl]], dma=True, desc=f"bwo{oc}")
                pp = pf[oc % 2]
                rp = r_pf[oc % 2]
                for dc in range(NDC):
                    E("T", lambda e, pp=pp, wsl=wsl, dc=dc, first=(dc == 0), last=(dc == 15):
                        e.matmul(pp[:, :], woutb[wsl][:, dc * 128:(dc + 1) * 128],
                                 z3[:, dc, :], start=first, stop=last),
                        reads=[r_woutb[wsl], r_z], writes=[rp], desc=f"op{oc}.{dc}")
                E("S", lambda e, pp=pp, oc=oc: e.activation(
                    aout[:, :], pp[:, :], AF.Identity, bias=bo_sb[:, oc:oc + 1]),
                    reads=[rp, r_const], writes=[r_aout], desc=f"oev{oc}")
                E("Y", lambda e, oc=oc, qb=qb: e.dma_start(
                    out=aod[oc * 128:(oc + 1) * 128, qb * TB:(qb + 1) * TB],
                    in_=aout[:, :]),
                    reads=[r_aout], writes=[r_aodQ[qb]], dma=True, desc=f"ost{oc}.{qb}")


    def emit_attnres(qb):
        stg = [norm, woutb[0], woutb[1]]
        rstg = [r_norm, r_woutb[0], r_woutb[1]]
        xst = [hq, habs]
        rxst = [r_hq, r_habs]
        for tt in range(4):
            gt = qb * 4 + tt
            ssl = gt % 3
            sl = gt % 2
            E("Y", lambda e, ssl=ssl, gt=gt: e.dma_start_transpose(
                stg[ssl][:, :], aod[:, gt * 128:(gt + 1) * 128]),
                reads=[r_aod], writes=[rstg[ssl]], dma=True, desc=f"atr{gt}")
            E("Y", lambda e, sl=sl, gt=gt: e.dma_start(
                out=xst[sl][:, 0:D], in_=x2d[gt * 128:(gt + 1) * 128, :]),
                reads=[r_x2dT[gt]], writes=[rxst[sl]], dma=True, desc=f"arx{gt}")
            E("V", lambda e, sl=sl, ssl=ssl: e.tensor_tensor(
                xst[sl][:, 0:D], xst[sl][:, 0:D], stg[ssl][:, :], ALU.add),
                reads=[rstg[ssl], rxst[sl]], writes=[rxst[sl]], desc=f"ares{gt}")
            E("Y", lambda e, sl=sl, gt=gt: e.dma_start(
                out=x3d[gt * 128:(gt + 1) * 128, :], in_=xst[sl][:, 0:D]),
                reads=[rxst[sl]], writes=[r_x3dT[gt]], dma=True, desc=f"asx{gt}")

    if last_phase == "ln0":
        emit_ln(xin, R("xin"), 0, "dbg")
        E("P", lambda e: e.dma_start(out=outx[0:128, 0:2048], in_=z[:, 0:2048]),
          reads=[r_z], writes=[r_outT[0]], dma=True, desc="dz0")
        E("P", lambda e: e.dma_start(out=outx[128:256, 0:2048], in_=z[:, 2048:4096]),
          reads=[r_z], writes=[r_outT[1]], dma=True, desc="dz1")
        E("P", lambda e: e.dma_start(out=outx[256:384, 0:2048], in_=z[:, 4096:6144]),
          reads=[r_z], writes=[r_outT[2]], dma=True, desc="dz2")
        E("P", lambda e: e.dma_start(out=outx[384:512, 0:2048], in_=z[:, 6144:8192]),
          reads=[r_z], writes=[r_outT[3]], dma=True, desc="dz3")
        E("Y", lambda e: e.dma_start(out=outx[512:640, 0:8], in_=stats[:, 0:8]),
          reads=r_stats2, writes=[r_outT[4]], dma=True, desc="dstats")
        sch.force_inc_last()
        with es:
            with nc.Block() as block:
                sch.finalize(nc, block)
        return nc

    if "ffn1" in phases:
        dst = (outx, r_outT) if last_phase == "ffn1" else (x2d, r_x2dT)
        emit_ffn(xin, r_xinT, dst[0], dst[1], "f1")
    if "qkv" in phases:
        emit_qkv()
    if "gather" in phases:
        for b in range(NTB):
            emit_gather(b)
    if "attn" in phases:
        emit_attn()
    if "ffn2" in phases:
        emit_ffn(x3d, r_x3dT, outx, r_outT, "f2",
                 res_in=(x2d, r_x2dT, aod, r_aodQ, x3d, r_x3dT))

    if last_phase in ("qkv", "gather", "attn"):
        for gt in range(TOK // 128):
            sl = gt % 2
            E("Y", lambda e, sl=sl, gt=gt: e.dma_start(
                out=xt[sl][:, :], in_=x2d[gt * 128:(gt + 1) * 128, :]),
                reads=[r_x2dT[gt]], writes=[r_xt[sl]], dma=True, desc=f"dbgl{gt}")
            E("Y", lambda e, sl=sl, gt=gt: e.dma_start(
                out=outx[gt * 128:(gt + 1) * 128, :], in_=xt[sl][:, :]),
                reads=[r_xt[sl]], writes=[r_outT[gt]], dma=True, desc=f"dbgs{gt}")
    if last_phase == "attnres":
        for gt in range(TOK // 128):
            sl = gt % 2
            E("Y", lambda e, sl=sl, gt=gt: e.dma_start(
                out=xt[sl][:, :], in_=x3d[gt * 128:(gt + 1) * 128, :]),
                reads=[r_x3dT[gt]], writes=[r_xt[sl]], dma=True, desc=f"dbgl{gt}")
            E("Y", lambda e, sl=sl, gt=gt: e.dma_start(
                out=outx[gt * 128:(gt + 1) * 128, :], in_=xt[sl][:, :]),
                reads=[r_xt[sl]], writes=[r_outT[gt]], dma=True, desc=f"dbgs{gt}")

    sch.force_inc_last()
    with es:
        with nc.Block() as block:
            sch.finalize(nc, block)
    return nc



def prep_inputs(L0_attn_x, L0_attn_Wqkv, L0_attn_bqkv, L0_attn_Wout, L0_attn_bout,
                L0_ffn_w_gate, L0_ffn_w_up, L0_ffn_w_down):
    x = np.asarray(L0_attn_x, dtype=np.float32)
    Wqkv = np.asarray(L0_attn_Wqkv, dtype=np.float32)
    bqkv = np.asarray(L0_attn_bqkv, dtype=np.float32)
    Wout = np.asarray(L0_attn_Wout, dtype=np.float32)
    bout = np.asarray(L0_attn_bout, dtype=np.float32)
    wg = np.asarray(L0_ffn_w_gate, dtype=np.float32)
    wu = np.asarray(L0_ffn_w_up, dtype=np.float32)
    wd = np.asarray(L0_ffn_w_down, dtype=np.float32)

    g_pan = _pt_lhsT(_perm_cols(wg))
    u_pan = _pt_lhsT(_perm_cols(wu))
    wgu = np.ascontiguousarray(np.concatenate([g_pan, u_pan], axis=2))
    wdn = _pt_lhsT(_perm_rows(wd))
    wqk = _pt_lhsT(Wqkv[:, 0:2560])
    wv = np.ascontiguousarray(Wqkv[:, 2560:3072].reshape(NDC, 128, KVW))
    wo = _pt_lhsT(Wout)
    bqk = np.ascontiguousarray(bqkv[0:2560].reshape(20, 128).T)
    bv = np.ascontiguousarray(bqkv[2560:3072].reshape(1, KVW))
    bo = np.ascontiguousarray(bout.reshape(16, 128).T)
    shared = dict(wgu=wgu, wdn=wdn, wqk=wqk, wv=wv, wo=wo, bqk=bqk, bv=bv, bo=bo,
                  ones_in=np.ones((128, 128), np.float32),
                  id_in=np.eye(128, dtype=np.float32))
    in_maps = []
    for c in range(8):
        b, hh = c // 2, c % 2
        xin = np.ascontiguousarray(x[b, hh * TOK:(hh + 1) * TOK, :])
        in_maps.append(dict(shared, xin=xin))
    return in_maps


def kernel(L0_attn_x, L0_attn_Wqkv, L0_attn_bqkv, L0_attn_Wout, L0_attn_bout,
           L0_ffn_w_gate, L0_ffn_w_up, L0_ffn_w_down):
    in_maps = prep_inputs(L0_attn_x, L0_attn_Wqkv, L0_attn_bqkv, L0_attn_Wout,
                          L0_attn_bout, L0_ffn_w_gate, L0_ffn_w_up, L0_ffn_w_down)
    nc = build_program("ffn2")
    res = run_bass_kernel_spmd(nc, in_maps, core_ids=list(range(8)))
    out = np.empty((4, SFULL, D), dtype=np.float32)
    for c in range(8):
        b, hh = c // 2, c % 2
        out[b, hh * TOK:(hh + 1) * TOK, :] = res.results[c]["outx"]
    return out


# version 10
# speedup vs baseline: 1.0327x; 1.0327x over previous
import numpy as np
import ml_dtypes
from contextlib import ExitStack

import concourse.bass as bass
import concourse.mybir as mybir
from concourse.bass_utils import run_bass_kernel_spmd

F32 = mybir.dt.float32
BF16 = mybir.dt.bfloat16
AF = mybir.ActivationFunctionType
ALU = mybir.AluOpType

D = 2048
TOK = 2048
SFULL = 4096
TB = 512
NTB = TOK // TB
NDC = D // 128
IFF = 8192
NIC = IFF // 128
NH = 16
NG = 4
HD = 128
KVW = 512
EPS = 1e-5


class _Res:
    __slots__ = ("name", "writers", "readers")
    def __init__(self, name):
        self.name = name
        self.writers = []
        self.readers = []

class _Op:
    __slots__ = ("eng", "fn", "deps", "dma", "cc", "seq", "inc", "cum", "desc",
                 "sem_key", "sem_val", "issue_wait")
    def __init__(self, eng, fn, dma, cc, desc):
        self.eng = eng
        self.fn = fn
        self.deps = []
        self.dma = dma
        self.cc = cc
        self.seq = -1
        self.inc = dma or cc
        self.cum = 0
        self.desc = desc
        self.sem_key = None
        self.sem_val = 0
        self.issue_wait = None

class Sched:
    ENGS = ("T", "S", "V", "P", "Y")

    def __init__(self):
        self.ops = {e: [] for e in self.ENGS}
        self.order = []

    def res(self, name):
        return _Res(name)

    def emit(self, eng, fn, reads=(), writes=(), dma=False, cc=False, desc=""):
        op = _Op(eng, fn, dma, cc, desc)
        op.seq = len(self.ops[eng])
        deps = {}

        def add_dep(d):
            if d is None or d is op:
                return
            if d.eng == eng == "T" and not (d.dma or d.cc or op.dma or op.cc):
                return
            if d.dma or d.cc:
                deps[id(d)] = d
            else:
                k = "e" + d.eng
                cur = deps.get(k)
                if cur is None or d.seq > cur.seq:
                    deps[k] = d

        def add_reader(lst, d):
            if d.dma or d.cc:
                lst.append(d)
            else:
                for i, x in enumerate(lst):
                    if not (x.dma or x.cc) and x.eng == d.eng:
                        if d.seq > x.seq:
                            lst[i] = d
                        return
                lst.append(d)

        for r in reads:
            for d in r.writers:
                add_dep(d)
            add_reader(r.readers, op)
        for w in writes:
            if w.readers or not (dma and all(x.dma for x in w.writers)):
                for d in w.writers:
                    add_dep(d)
                for rd in w.readers:
                    add_dep(rd)
                w.writers = [op]
                w.readers = []
            else:
                w.writers.append(op)
        op.deps = list(deps.values())
        self.ops[eng].append(op)
        self.order.append(op)
        return op

    NPOOL = 8

    def finalize(self, nc, block):
        for op in self.order:
            for d in op.deps:
                d.inc = True
        sem_names = ["T", "S", "V"]
        for e in ("Y", "P"):
            sem_names += [f"{e}{i}" for i in range(self.NPOOL)]
        sem_names += ["C0", "C1"]
        es = ExitStack()
        sems = {n: es.enter_context(nc.semaphore(f"sem{n}")) for n in sem_names}
        self._es = es

        for e in ("T", "S", "V"):
            c = 0
            for op in self.ops[e]:
                if op.inc:
                    c += 1
                op.cum = c
                op.sem_key = e
                op.sem_val = c
        for e in ("Y", "P"):
            ndma = 0
            ncc = 0
            uses = {}
            for op in self.ops[e]:
                if op.dma:
                    key = f"{e}{ndma % self.NPOOL}"
                    ndma += 1
                elif op.cc:
                    key = f"C{ncc % 2}"
                    ncc += 1
                else:
                    raise AssertionError(f"non-dma non-cc op on queue {e}: {op.desc}")
                u = uses.get(key, 0)
                uses[key] = u + 1
                step = 16 if op.dma else 1
                op.sem_key = key
                op.sem_val = (u + 1) * step
                if u > 0:
                    op.issue_wait = (key, u * step)

        def mk_stream(ename):
            def stream(eng):
                waited = {}
                def wait(key, v):
                    if waited.get(key, 0) < v:
                        eng.wait_ge(sems[key], v)
                        waited[key] = v
                for op in self.ops[ename]:
                    if op.issue_wait is not None:
                        wait(*op.issue_wait)
                    byk = {}
                    for d in op.deps:
                        byk[d.sem_key] = max(byk.get(d.sem_key, 0), d.sem_val)
                    for k, v in byk.items():
                        wait(k, v)
                    inst = op.fn(eng)
                    if op.inc:
                        inst.then_inc(sems[op.sem_key], 16 if op.dma else 1)
                if ename == "Y":
                    fin = {}
                    for op in self.ops["Y"]:
                        if op.dma:
                            fin[op.sem_key] = max(fin.get(op.sem_key, 0), op.sem_val)
                    for k, v in fin.items():
                        wait(k, v)
            return stream

        block.tensor(mk_stream("T"))
        block.scalar(mk_stream("S"))
        block.vector(mk_stream("V"))
        block.gpsimd(mk_stream("P"))
        block.sync(mk_stream("Y"))

    def force_inc_last(self):
        for e in self.ENGS:
            if self.ops[e]:
                self.ops[e][-1].inc = True



def _pt_lhsT(w):
    Dd, O = w.shape
    return np.ascontiguousarray(
        w.reshape(Dd // 128, 128, O // 128, 128).transpose(2, 1, 0, 3)
         .reshape(O // 128, 128, (Dd // 128) * 128))

def _perm_cols(w):
    Dd, I = w.shape
    return np.ascontiguousarray(w.reshape(Dd, I // 4, 4).transpose(0, 2, 1).reshape(Dd, I))

def _perm_rows(w):
    I, Dd = w.shape
    return np.ascontiguousarray(w.reshape(I // 4, 4, Dd).transpose(1, 0, 2).reshape(I, Dd))



def build_program(last_phase="ffn2"):
    nc = bass.Bass("TRN2", target_bir_lowering=False)
    sch = Sched()
    PH = ["ffn1", "qkv", "gather", "attn", "attnres", "ffn2"]
    phases = [] if last_phase == "ln0" else PH[: PH.index(last_phase) + 1]

    xin = nc.dram_tensor("xin", [TOK, D], F32, kind="ExternalInput").ap()
    wgu = nc.dram_tensor("wgu", [NIC, 128, 2 * D], F32, kind="ExternalInput").ap()
    wdn = nc.dram_tensor("wdn", [NDC, 128, IFF], F32, kind="ExternalInput").ap()
    wqk = nc.dram_tensor("wqk", [20, 128, D], F32, kind="ExternalInput").ap()
    wv = nc.dram_tensor("wv", [NDC, 128, KVW], F32, kind="ExternalInput").ap()
    wo = nc.dram_tensor("wo", [NDC, 128, D], F32, kind="ExternalInput").ap()
    bqk_in = nc.dram_tensor("bqk", [128, 20], F32, kind="ExternalInput").ap()
    bv_in = nc.dram_tensor("bv", [1, KVW], F32, kind="ExternalInput").ap()
    bo_in = nc.dram_tensor("bo", [128, 16], F32, kind="ExternalInput").ap()
    ones_in = nc.dram_tensor("ones_in", [128, 128], F32, kind="ExternalInput").ap()
    id_in = nc.dram_tensor("id_in", [128, 128], F32, kind="ExternalInput").ap()
    outx = nc.dram_tensor("outx", [TOK, D], F32, kind="ExternalOutput").ap()

    x2d = nc.dram_tensor("x2d", [TOK, D], F32).ap()
    x3d = nc.dram_tensor("x3d", [TOK, D], F32).ap()
    aod = nc.dram_tensor("aod", [D, TOK], BF16).ap()
    wguc = nc.dram_tensor("wguc", [NIC, 128, 2 * D], BF16).ap()
    wdnc = nc.dram_tensor("wdnc", [NDC, 128, IFF], BF16).ap()
    wqkc = nc.dram_tensor("wqkc", [20, 128, D], BF16).ap()
    woc = nc.dram_tensor("woc", [NDC, 128, D], BF16).ap()
    ksend_t = nc.dram_tensor("ksend", [NTB, KVW, TB], BF16)
    krecv_t = nc.dram_tensor("krecv", [NTB, 2 * KVW, TB], BF16)
    vsend_t = nc.dram_tensor("vsend", [NTB, TB, KVW], BF16)
    vrecv_t = nc.dram_tensor("vrecv", [NTB, 2 * TB, KVW], BF16)
    ksend, krecv = ksend_t.ap(), krecv_t.ap()
    vsend, vrecv = vsend_t.ap(), vrecv_t.ap()

    es = ExitStack()
    sb = lambda name, shape, dt: es.enter_context(nc.sbuf_tensor(name, shape, dt))
    ps = lambda name, shape, dt: es.enter_context(nc.psum_tensor(name, shape, dt))

    xt = [sb(f"xt{i}", [128, D], F32) for i in range(2)]
    z = sb("z", [128, NDC * TB], BF16)
    hsq = sb("hsq", [128, 64 * 512], BF16)
    warn = sb("warn", [128, 64 * 512 // 2 * 2], BF16)
    hq = sb("hq", [128, 4 * TB], F32)
    habs = sb("habs", [128, 4 * TB], F32)
    norm = sb("norm", [128, D], BF16)
    woutb = [sb(f"woutb{i}", [128, D], BF16) for i in range(2)]
    aout = sb("aout", [128, TB], BF16)
    pt = sb("ptile", [128, 8 * 512], BF16)
    rcp = sb("rcp", [128, 512], F32)
    spt = sb("spt", [128, 3 * TB], F32)
    stats = sb("stats", [128, 16], F32)
    bqk_sb = sb("bqk_sb", [128, 20], F32)
    bo_sb = sb("bo_sb", [128, 16], F32)
    bv_sb = sb("bv_sb", [1, KVW], BF16)
    ones = sb("ones", [128, 128], BF16)
    ident = sb("ident", [128, 128], BF16)

    pApair = ps("pA", [128, 1024], F32)
    pBpair = ps("pB", [128, 1024], F32)
    pC = ps("pC", [128, 512], F32)
    pD = ps("pD", [128, 512], F32)
    pf = [pC[:, :], pD[:, :], pApair[:, 0:512], pBpair[:, 0:512],
          pApair[:, 512:1024], pBpair[:, 512:1024]]
    pb = [ps("pb0", [128, 1024], BF16)[:, :], ps("pb1", [128, 1024], BF16)[:, :]]

    R = sch.res
    r_xt = [R(f"xt{i}") for i in range(2)]
    r_z = R("z")
    r_hs = [R(f"hs{i}") for i in range(64)]
    r_wz = [R(f"wz{i}") for i in range(8)]
    r_hqk = [R(f"hq{i}") for i in range(4)]
    r_habsk = [R(f"habs{i}") for i in range(4)]
    r_hq = R("hq")
    r_habs = R("habs")
    r_norm = R("norm")
    r_woutb = [R(f"woutb{i}") for i in range(2)]
    r_aout = R("aout")
    r_pt = [R(f"pt{i}") for i in range(3)]
    r_ptq = [R(f"ptq{i}") for i in range(2)]
    r_rcp = R("rcp")
    r_spt = R("spt")
    r_stats2 = [R("statsA"), R("statsB")]
    r_pf = [R(f"pf{i}") for i in range(6)]
    r_pb = [R(f"pb{i}") for i in range(2)]
    r_x2dT = [R(f"x2d{i}") for i in range(16)]
    r_x3dT = [R(f"x3d{i}") for i in range(16)]
    r_outT = [R(f"out{i}") for i in range(16)]
    r_xinT = [R("xin")] * 16
    r_aod = R("aod")
    r_ksend = [R(f"ksend{i}") for i in range(NTB)]
    r_krecv = [R(f"krecv{i}") for i in range(NTB)]
    r_vsend = [R(f"vsend{i}") for i in range(NTB)]
    r_vrecv = [R(f"vrecv{i}") for i in range(NTB)]
    r_const = R("const")
    r_wguc = R("wguc")
    r_wdnc = R("wdnc")
    r_wqkc = R("wqkc")
    r_woc = R("woc")

    E = sch.emit

    E("Y", lambda e: e.dma_start(out=bqk_sb[:, :], in_=bqk_in), writes=[r_const], dma=True, desc="bqk")
    E("Y", lambda e: e.dma_start(out=bo_sb[:, :], in_=bo_in), writes=[r_const], dma=True, desc="bo")
    E("P", lambda e: e.dma_start(out=bv_sb[:, :], in_=bv_in), writes=[r_const], dma=True, desc="bv")
    E("P", lambda e: e.dma_start(out=ones[:, :], in_=ones_in), writes=[r_const], dma=True, desc="ones")
    E("P", lambda e: e.dma_start(out=ident[:, :], in_=id_in), writes=[r_const], dma=True, desc="ident")

    z3 = z[:, :].rearrange("p (c n) -> p c n", n=TB)
    hs3 = hsq[:, :].rearrange("p (c n) -> p c n", n=512)
    w_all = warn[:, :]

    def wzone(i0, n, shape_cols):
        apv = warn[:, i0 * 4096:(i0 + n) * 4096]
        return apv

    def emit_ln(src_ap, r_srcT, b, lnname):
        for t in range(4):
            sl = t % 2
            gt = b * 4 + t
            E("Y", lambda e, sl=sl, gt=gt: e.dma_start(
                out=xt[sl][:, :], in_=src_ap[gt * 128:(gt + 1) * 128, :]),
                reads=[r_srcT[gt]], writes=[r_xt[sl]], dma=True, desc=f"{lnname} ldx{gt}")
            c0 = (t % 2) * 4
            r_stats = r_stats2[t % 2]
            E("S", lambda e, sl=sl, c0=c0: e.activation(
                norm[:, :], xt[sl][:, :], AF.Identity, accum_out=stats[:, c0:c0 + 1]),
                reads=[r_xt[sl]], writes=[r_norm, r_stats], desc=f"{lnname} sum{t}")
            E("S", lambda e, sl=sl, c0=c0: e.activation(
                norm[:, :], xt[sl][:, :], AF.Square, accum_out=stats[:, c0 + 1:c0 + 2]),
                reads=[r_xt[sl]], writes=[r_norm, r_stats], desc=f"{lnname} sq{t}")
            E("V", lambda e, c0=c0: e.tensor_scalar(
                stats[:, c0 + 2:c0 + 3], stats[:, c0:c0 + 1], -1.0 / D, None, ALU.mult),
                reads=[r_stats], writes=[r_stats], desc="negmean")
            E("V", lambda e, c0=c0: e.tensor_scalar(
                stats[:, c0 + 3:c0 + 4], stats[:, c0 + 1:c0 + 2], 1.0 / D, EPS, ALU.mult, ALU.add),
                reads=[r_stats], writes=[r_stats], desc="ex2e")
            E("V", lambda e, c0=c0: e.tensor_tensor(
                stats[:, c0:c0 + 1], stats[:, c0 + 2:c0 + 3], stats[:, c0 + 2:c0 + 3], ALU.mult),
                reads=[r_stats], writes=[r_stats], desc="msq")
            E("V", lambda e, c0=c0: e.tensor_tensor(
                stats[:, c0 + 1:c0 + 2], stats[:, c0 + 3:c0 + 4], stats[:, c0:c0 + 1], ALU.subtract),
                reads=[r_stats], writes=[r_stats], desc="vpe")
            E("S", lambda e, c0=c0: e.activation(
                stats[:, c0:c0 + 1], stats[:, c0 + 1:c0 + 2], AF.Ln),
                reads=[r_stats], writes=[r_stats], desc="ln")
            E("S", lambda e, c0=c0: e.activation(
                stats[:, c0 + 1:c0 + 2], stats[:, c0:c0 + 1], AF.Exp, scale=-0.5),
                reads=[r_stats], writes=[r_stats], desc="rstd")
            E("V", lambda e, sl=sl, c0=c0: e.tensor_scalar(
                norm[:, :], xt[sl][:, :], stats[:, c0 + 2:c0 + 3], stats[:, c0 + 1:c0 + 2],
                ALU.add, ALU.mult),
                reads=[r_xt[sl], r_stats], writes=[r_norm], desc=f"{lnname} nrm{t}")
            for h in range(2):
                pbi = 1
                for j in range(8):
                    dc = h * 8 + j
                    E("T", lambda e, pbi=pbi, j=j, dc=dc, last=(j == 7): e.transpose(
                        pb[pbi][:, j * 128:(j + 1) * 128], norm[:, dc * 128:(dc + 1) * 128],
                        ident[:, :]),
                        reads=[r_norm, r_const], writes=[r_pb[pbi]], desc=f"tr{dc}")
                E("V", lambda e, pbi=pbi, h=h, t=t: e.tensor_copy(
                    z3[:, h * 8:(h + 1) * 8, t * 128:(t + 1) * 128],
                    pb[pbi][:, :].rearrange("p (c n) -> p c n", n=128)),
                    reads=[r_pb[pbi]], writes=[r_z], desc=f"zcp{t}.{h}")

    def emit_ffn(src_ap, r_srcT, dst_ap, r_dstT, tag):
        def emit_gu_load(c, cold):
            wsl = c % 3
            if cold:
                E("P", lambda e, c=c, wsl=wsl: e.dma_start(
                    out=warn[:, wsl * 4096:(wsl + 1) * 4096], in_=wgu[c]),
                    writes=[r_wz[wsl]], dma=True, desc=f"{tag}wgu{c}")
                E("Y", lambda e, c=c, wsl=wsl: e.dma_start(
                    out=wguc[c], in_=warn[:, wsl * 4096:(wsl + 1) * 4096]),
                    reads=[r_wz[wsl]], writes=[r_wguc], dma=True, desc=f"cgu{c}")
            else:
                E("P", lambda e, c=c, wsl=wsl: e.dma_start(
                    out=warn[:, wsl * 4096:(wsl + 1) * 4096], in_=wguc[c]),
                    reads=[r_wguc], writes=[r_wz[wsl]], dma=True, desc=f"{tag}bgu{c}")

        chunk_of = [k * 16 + q for q in range(16) for k in range(4)]
        for b in range(NTB):
            E("P", lambda e, b=b: e.dma_start(
                out=dst_ap[b * TB:(b + 1) * TB, :], in_=src_ap[b * TB:(b + 1) * TB, :]),
                reads=[r_srcT[i] for i in range(b * 4, b * 4 + 4)],
                writes=[r_dstT[i] for i in range(b * 4, b * 4 + 4)],
                dma=True, desc=f"{tag}pre{b}")
            emit_ln(src_ap, r_srcT, b, f"{tag}ln{b}")
            cold = (tag == "f1" and b == 0)
            next_load = 0 if b == 0 else 3
            for q in range(16):
                for k in range(4):
                    c = k * 16 + q
                    i = q * 4 + k
                    while next_load <= min(i + 2, 63):
                        emit_gu_load(chunk_of[next_load], cold)
                        next_load += 1
                    wsl = c % 3
                    gb = [0, 1][(q * 4 + k) % 2]
                    ub = [2, 3][(q * 4 + k) % 2]
                    pg, pu = pf[gb], pf[ub]
                    rg, ru = r_pf[gb], r_pf[ub]
                    for dc in range(NDC):
                        E("T", lambda e, pg=pg, wsl=wsl, dc=dc, first=(dc == 0), last=(dc == 15):
                            e.matmul(pg[:, :], warn[:, wsl * 4096 + dc * 128: wsl * 4096 + (dc + 1) * 128],
                                     z3[:, dc, :], start=first, stop=last),
                            reads=[r_wz[wsl], r_z], writes=[rg], desc=f"g{c}.{dc}")
                    for dc in range(NDC):
                        E("T", lambda e, pu=pu, wsl=wsl, dc=dc, first=(dc == 0), last=(dc == 15):
                            e.matmul(pu[:, :], warn[:, wsl * 4096 + 2048 + dc * 128: wsl * 4096 + 2048 + (dc + 1) * 128],
                                     z3[:, dc, :], start=first, stop=last),
                            reads=[r_wz[wsl], r_z], writes=[ru], desc=f"u{c}.{dc}")
                    E("S", lambda e, pg=pg, k=k: e.activation(
                        hq[:, k * TB:(k + 1) * TB], pg[:, :], AF.Silu),
                        reads=[rg], writes=[r_hqk[k]], desc=f"silu{c}")
                    E("V", lambda e, pu=pu, k=k: e.tensor_tensor(
                        hq[:, k * TB:(k + 1) * TB], hq[:, k * TB:(k + 1) * TB], pu[:, :], ALU.mult),
                        reads=[ru, r_hqk[k]], writes=[r_hqk[k]], desc=f"hmul{c}")
                    E("S", lambda e, k=k: e.activation(
                        habs[:, k * TB:(k + 1) * TB], hq[:, k * TB:(k + 1) * TB], AF.Abs),
                        reads=[r_hqk[k]], writes=[r_habsk[k]], desc=f"abs{c}")
                    if k == 1:
                        E("V", lambda e: e.tensor_tensor(
                            spt[:, 0:TB], habs[:, 0:TB], habs[:, TB:2 * TB], ALU.max),
                            reads=[r_habsk[0], r_habsk[1]], writes=[r_spt], desc="max01")
                        E("V", lambda e: e.tensor_tensor(
                            spt[:, TB:2 * TB], habs[:, 0:TB], habs[:, TB:2 * TB], ALU.min),
                            reads=[r_habsk[0], r_habsk[1]], writes=[r_spt], desc="min01")
                A = lambda k: habs[:, k * TB:(k + 1) * TB]
                t0, t1, t2 = spt[:, 0:TB], spt[:, TB:2 * TB], spt[:, 2 * TB:3 * TB]
                seq = [
                    (t2, A(2), A(3), ALU.max, [r_habsk[2], r_habsk[3]]),
                    (t0, t0, t2, ALU.min, []),
                    (t2, A(2), A(3), ALU.min, [r_habsk[2], r_habsk[3]]),
                    (t1, t1, t2, ALU.max, []),
                    (t1, t0, t1, ALU.max, []),
                ]
                for (o, i0, i1, opk, xr) in seq:
                    E("V", lambda e, o=o, i0=i0, i1=i1, opk=opk: e.tensor_tensor(o, i0, i1, opk),
                        reads=[r_spt] + xr, writes=[r_spt], desc="m2")
                for k in range(4):
                    c = k * 16 + q
                    E("V", lambda e, k=k: e.tensor_tensor(
                        A(k), A(k), spt[:, TB:2 * TB], ALU.is_ge),
                        reads=[r_habsk[k], r_spt], writes=[r_habsk[k]], desc=f"ge{c}")
                    E("V", lambda e, k=k, c=c: e.tensor_tensor(
                        hs3[:, c, :], hq[:, k * TB:(k + 1) * TB], A(k), ALU.mult),
                        reads=[r_habsk[k], r_hqk[k]], writes=[r_hs[c]], desc=f"hs{c}")
            for oc in range(NDC):
                dsl = 3 + (oc % 2) * 2
                if cold:
                    E("P", lambda e, oc=oc, dsl=dsl: e.dma_start(
                        out=warn[:, dsl * 4096:(dsl + 2) * 4096], in_=wdn[oc]),
                        writes=[r_wz[dsl], r_wz[dsl + 1]], dma=True, desc=f"{tag}wdn{oc}")
                    E("Y", lambda e, oc=oc, dsl=dsl: e.dma_start(
                        out=wdnc[oc], in_=warn[:, dsl * 4096:(dsl + 2) * 4096]),
                        reads=[r_wz[dsl], r_wz[dsl + 1]], writes=[r_wdnc], dma=True,
                        desc=f"cdn{oc}")
                else:
                    E("P", lambda e, oc=oc, dsl=dsl: e.dma_start(
                        out=warn[:, dsl * 4096:(dsl + 2) * 4096], in_=wdnc[oc]),
                        reads=[r_wdnc], writes=[r_wz[dsl], r_wz[dsl + 1]], dma=True,
                        desc=f"{tag}bdn{oc}")
                pd = pf[[0, 1, 2][oc % 3]]
                rd = r_pf[[0, 1, 2][oc % 3]]
                ics = [q2 + 16 * k2 for q2 in range(16) for k2 in range(4)]
                for idx, ic in enumerate(ics):
                    E("T", lambda e, pd=pd, dsl=dsl, ic=ic, first=(idx == 0), last=(idx == 63):
                        e.matmul(pd[:, :], warn[:, dsl * 4096 + ic * 128: dsl * 4096 + (ic + 1) * 128],
                                 hs3[:, ic, :], start=first, stop=last),
                        reads=[r_wz[dsl], r_wz[dsl + 1], r_hs[ic]], writes=[rd], desc=f"d{oc}.{ic}")
                E("S", lambda e, pd=pd: e.mul(aout[:, :], pd[:, :], 0.5),
                    reads=[rd], writes=[r_aout], desc=f"dev{oc}")
                pbi = 0
                for tt in range(4):
                    E("T", lambda e, pbi=pbi, tt=tt: e.transpose(
                        pb[pbi][:, tt * 128:(tt + 1) * 128],
                        aout[:, tt * 128:(tt + 1) * 128], ident[:, :]),
                        reads=[r_aout, r_const], writes=[r_pb[pbi]], desc=f"dtr{oc}.{tt}")
                E("V", lambda e, pbi=pbi, oc=oc: e.tensor_copy(
                    z[:, :].rearrange("p (t n) -> p t n", n=D)[:, 0:4, oc * 128:(oc + 1) * 128],
                    pb[pbi][:, 0:512].rearrange("p (t n) -> p t n", n=128)),
                    reads=[r_pb[pbi]], writes=[r_z], desc=f"dcp{oc}")
            if b + 1 < NTB:
                ncold = (tag == "f1" and b + 1 == 0)
                for j in range(3):
                    emit_gu_load(chunk_of[j], ncold)
            ztok = z[:, :].rearrange("p (t n) -> p t n", n=D)
            for tt in range(4):
                gt = b * 4 + tt
                E("P", lambda e, tt=tt, gt=gt: e.dma_start(
                    out=dst_ap[gt * 128:(gt + 1) * 128, :], in_=ztok[:, tt, :],
                    accum_op=ALU.add),
                    reads=[r_z, r_dstT[gt]], writes=[r_dstT[gt]], dma=True,
                    desc=f"{tag}acc{gt}")

    def emit_qkv():
        for dc in range(NDC):
            E("P", lambda e, dc=dc: e.dma_start(
                out=warn[:, dc * 512:(dc + 1) * 512], in_=wv[dc]),
                writes=[r_wz[dc // 8]], dma=True, desc=f"wv{dc}")
        wv3 = warn[:, 0:NDC * 512].rearrange("p (c n) -> p c n", n=512)
        for b in range(NTB):
            emit_ln(x2d, r_x2dT, b, f"qln{b}")
            for oc in range(20):
                wsl = 4 + (oc % 2)
                if b == 0:
                    E("P", lambda e, oc=oc, wsl=wsl: e.dma_start(
                        out=warn[:, wsl * 4096: wsl * 4096 + 2048], in_=wqk[oc]),
                        writes=[r_wz[wsl]], dma=True, desc=f"wqk{oc}")
                    E("Y", lambda e, oc=oc, wsl=wsl: e.dma_start(
                        out=wqkc[oc], in_=warn[:, wsl * 4096: wsl * 4096 + 2048]),
                        reads=[r_wz[wsl]], writes=[r_wqkc], dma=True, desc=f"cqk{oc}")
                else:
                    E("P", lambda e, oc=oc, wsl=wsl: e.dma_start(
                        out=warn[:, wsl * 4096: wsl * 4096 + 2048], in_=wqkc[oc]),
                        reads=[r_wqkc], writes=[r_wz[wsl]], dma=True, desc=f"bqk{oc}")
                pq = pf[oc % 4]
                rq = r_pf[oc % 4]
                for dc in range(NDC):
                    E("T", lambda e, pq=pq, wsl=wsl, dc=dc, first=(dc == 0), last=(dc == 15):
                        e.matmul(pq[:, :], warn[:, wsl * 4096 + dc * 128: wsl * 4096 + (dc + 1) * 128],
                                 z3[:, dc, :], start=first, stop=last),
                        reads=[r_wz[wsl], r_z], writes=[rq], desc=f"qk{oc}.{dc}"),
                if oc < 16:
                    E("S", lambda e, pq=pq, oc=oc, b=b: e.activation(
                        hsq[:, oc * 2048 + b * TB: oc * 2048 + (b + 1) * TB], pq[:, :],
                        AF.Identity, bias=bqk_sb[:, oc:oc + 1]),
                        reads=[rq, r_const], writes=[r_hs[oc]], desc=f"qev{oc}"),
                else:
                    g = oc - 16
                    psl = (oc + b) % 2
                    E("S", lambda e, pq=pq, oc=oc, psl=psl: e.activation(
                        pt[:, 3072 + psl * 512: 3072 + (psl + 1) * 512], pq[:, :],
                        AF.Identity, bias=bqk_sb[:, oc:oc + 1]),
                        reads=[rq, r_const], writes=[r_ptq[psl]], desc=f"kev{oc}"),
                    E("Y", lambda e, g=g, b=b, psl=psl: e.dma_start(
                        out=ksend[b, g * 128:(g + 1) * 128, :],
                        in_=pt[:, 3072 + psl * 512: 3072 + (psl + 1) * 512]),
                        reads=[r_ptq[psl]], writes=[r_ksend[b]], dma=True, desc=f"kst{g}.{b}")
            for tt in range(4):
                pv = pf[4 + tt % 2]
                rv = r_pf[4 + tt % 2]
                for dc in range(NDC):
                    E("T", lambda e, pv=pv, dc=dc, tt=tt, first=(dc == 0):
                        e.matmul(pv[:, :], z3[:, dc, tt * 128:(tt + 1) * 128],
                                 wv3[:, dc, :], start=first, stop=False),
                        reads=[r_wz[0], r_wz[1], r_z], writes=[rv], desc=f"v{tt}.{dc}")
                E("T", lambda e, pv=pv: e.matmul(
                    pv[:, :], ones[0:1, :], bv_sb[:, :], start=False, stop=True),
                    reads=[r_const], writes=[rv], desc=f"vb{tt}")
                psl = tt % 2
                E("S", lambda e, pv=pv, psl=psl: e.activation(
                    pt[:, 3072 + psl * 512: 3072 + (psl + 1) * 512], pv[:, :], AF.Identity),
                    reads=[rv], writes=[r_ptq[psl]], desc=f"vev{tt}")
                E("Y", lambda e, b=b, tt=tt, psl=psl: e.dma_start(
                    out=vsend[b, tt * 128:(tt + 1) * 128, :],
                    in_=pt[:, 3072 + psl * 512: 3072 + (psl + 1) * 512]),
                    reads=[r_ptq[psl]], writes=[r_vsend[b]], dma=True, desc=f"vst{b}.{tt}")
            emit_gather_block(b)

    GROUPS = [[0, 1], [2, 3], [4, 5], [6, 7]]

    def emit_gather_block(b):
        E("P", lambda e, b=b: e.collective_compute(
            "AllGather", ALU.bypass, replica_groups=GROUPS,
            ins=[ksend_t.ap()[b].opt()], outs=[krecv_t.ap()[b].opt()]),
            reads=[r_ksend[b]], writes=[r_krecv[b]], cc=True, desc=f"ccK{b}")
        E("P", lambda e, b=b: e.collective_compute(
            "AllGather", ALU.bypass, replica_groups=GROUPS,
            ins=[vsend_t.ap()[b].opt()], outs=[vrecv_t.ap()[b].opt()]),
            reads=[r_vsend[b]], writes=[r_vrecv[b]], cc=True, desc=f"ccV{b}")

    def emit_gather(b):
        for g in range(NG):
            for r in range(2):
                E("Y", lambda e, g=g, r=r, b=b: e.dma_start(
                    out=warn[:, g * 4096 + r * 2048 + b * TB: g * 4096 + r * 2048 + (b + 1) * TB],
                    in_=krecv[b, r * KVW + g * 128: r * KVW + (g + 1) * 128, :]),
                    reads=[r_krecv[b]], writes=[r_wz[g]], dma=True, desc=f"kld{g}.{r}.{b}")
        for r in range(2):
            base = 4 * 4096 + (r * 16 + b * 4) * 512
            E("Y", lambda e, r=r, b=b, base=base: e.dma_start(
                out=warn[:, base: base + 4 * 512].rearrange("p (t n) -> p t n", n=512),
                in_=vrecv[b, r * TB:(r + 1) * TB, :].rearrange("(t p) n -> p t n", p=128)),
                reads=[r_vrecv[b]], writes=[r_wz[4 + (r * 16 + b * 4) // 8]],
                dma=True, desc=f"vld{r}.{b}")

    def emit_attn():
        k3 = warn[:, 0:4 * 4096].rearrange("p (g n) -> p g n", n=4096)
        v3 = warn[:, 4 * 4096:8 * 4096].rearrange("p (t n) -> p t n", n=512)
        q3 = hsq[:, :].rearrange("p (h n) -> p h n", n=2048)
        NKT = SFULL // 128
        for qb in range(NTB):
            for h in range(NH):
                g = h % NG
                po = pf[0]
                pr = pf[1]
                ro, rr = r_pf[0], r_pf[1]
                ST_BANKS = [2, 4, 3, 5]
                def st_op(kt):
                    bank = ST_BANKS[kt % 4]
                    E("T", lambda e, kt=kt, bank=bank, g=g, h=h, qb=qb: e.matmul(
                        pf[bank][:, :], k3[:, g, kt * 128:(kt + 1) * 128],
                        q3[:, h, qb * TB:(qb + 1) * TB], start=True, stop=True),
                        reads=[r_wz[g], r_hs[h]], writes=[r_pf[bank]], desc=f"st{h}.{kt}")
                for kt in range(4):
                    st_op(kt)
                for kp in range(NKT // 2):
                    k0 = 2 * kp
                    pair_t = pApair if (k0 % 4) == 0 else pBpair
                    prs = [r_pf[2], r_pf[4]] if (k0 % 4) == 0 else [r_pf[3], r_pf[5]]
                    psl = kp % 3
                    E("S", lambda e, pair_t=pair_t, psl=psl: e.activation(
                        pt[:, psl * 1024:(psl + 1) * 1024],
                        pair_t[:, :], AF.Exp, scale=1.0 / HD),
                        reads=prs, writes=[r_pt[psl]], desc=f"exp{h}.{kp}")
                    if k0 + 5 < NKT:
                        st_op(k0 + 4)
                        st_op(k0 + 5)
                    for j in range(2):
                        kt = k0 + j
                        pslc = pt[:, psl * 1024 + j * 512: psl * 1024 + (j + 1) * 512]
                        E("T", lambda e, kt=kt, g=g, pslc=pslc, first=(kt == 0), last=(kt == NKT - 1):
                            e.matmul(po[:, :], v3[:, kt, g * 128:(g + 1) * 128],
                                     pslc, start=first, stop=last),
                            reads=[r_wz[4 + kt // 8], r_pt[psl]], writes=[ro], desc=f"pv{h}.{kt}")
                        E("T", lambda e, kt=kt, pslc=pslc, first=(kt == 0), last=(kt == NKT - 1):
                            e.matmul(pr[:, :], ones[:, :], pslc, start=first, stop=last),
                            reads=[r_const, r_pt[psl]], writes=[rr], desc=f"rs{h}.{kt}")
                E("V", lambda e, pr=pr: e.reciprocal(rcp[:, :], pr[:, :]),
                    reads=[rr], writes=[r_rcp], desc=f"rcp{h}")
                E("V", lambda e, po=po, h=h: e.tensor_tensor(
                    z3[:, h, :], po[:, :], rcp[:, :], ALU.mult),
                    reads=[ro, r_rcp], writes=[r_z], desc=f"at{h}")
            for oc in range(NDC):
                wsl = oc % 2
                if qb == 0:
                    E("P", lambda e, oc=oc, wsl=wsl: e.dma_start(
                        out=woutb[wsl][:, :], in_=wo[oc]),
                        writes=[r_woutb[wsl]], dma=True, desc=f"wo{oc}")
                    E("Y", lambda e, oc=oc, wsl=wsl: e.dma_start(
                        out=woc[oc], in_=woutb[wsl][:, :]),
                        reads=[r_woutb[wsl]], writes=[r_woc], dma=True, desc=f"cwo{oc}")
                else:
                    E("P", lambda e, oc=oc, wsl=wsl: e.dma_start(
                        out=woutb[wsl][:, :], in_=woc[oc]),
                        reads=[r_woc], writes=[r_woutb[wsl]], dma=True, desc=f"bwo{oc}")
                pp = pf[oc % 2]
                rp = r_pf[oc % 2]
                for dc in range(NDC):
                    E("T", lambda e, pp=pp, wsl=wsl, dc=dc, first=(dc == 0), last=(dc == 15):
                        e.matmul(pp[:, :], woutb[wsl][:, dc * 128:(dc + 1) * 128],
                                 z3[:, dc, :], start=first, stop=last),
                        reads=[r_woutb[wsl], r_z], writes=[rp], desc=f"op{oc}.{dc}")
                E("S", lambda e, pp=pp, oc=oc: e.activation(
                    aout[:, :], pp[:, :], AF.Identity, bias=bo_sb[:, oc:oc + 1]),
                    reads=[rp, r_const], writes=[r_aout], desc=f"oev{oc}")
                E("Y", lambda e, oc=oc, qb=qb: e.dma_start(
                    out=aod[oc * 128:(oc + 1) * 128, qb * TB:(qb + 1) * TB],
                    in_=aout[:, :]),
                    reads=[r_aout], writes=[r_aod], dma=True, desc=f"ost{oc}.{qb}")
            emit_attnres(qb)

    def emit_attnres(qb):
        stg = [norm, woutb[0], woutb[1]]
        rstg = [r_norm, r_woutb[0], r_woutb[1]]
        xst = [hq, habs]
        rxst = [[r_hq] + r_hqk, [r_habs] + r_habsk]
        for tt in range(4):
            gt = qb * 4 + tt
            ssl = gt % 3
            sl = gt % 2
            E("Y", lambda e, ssl=ssl, gt=gt: e.dma_start_transpose(
                stg[ssl][:, :], aod[:, gt * 128:(gt + 1) * 128]),
                reads=[r_aod], writes=[rstg[ssl]], dma=True, desc=f"atr{gt}")
            E("Y", lambda e, sl=sl, gt=gt: e.dma_start(
                out=xst[sl][:, 0:D], in_=x2d[gt * 128:(gt + 1) * 128, :]),
                reads=[r_x2dT[gt]], writes=rxst[sl], dma=True, desc=f"arx{gt}")
            E("V", lambda e, sl=sl, ssl=ssl: e.tensor_tensor(
                xst[sl][:, 0:D], xst[sl][:, 0:D], stg[ssl][:, :], ALU.add),
                reads=[rstg[ssl]] + rxst[sl], writes=rxst[sl], desc=f"ares{gt}")
            E("Y", lambda e, sl=sl, gt=gt: e.dma_start(
                out=x3d[gt * 128:(gt + 1) * 128, :], in_=xst[sl][:, 0:D]),
                reads=rxst[sl], writes=[r_x3dT[gt]], dma=True, desc=f"asx{gt}")

    if last_phase == "ln0":
        emit_ln(xin, R("xin"), 0, "dbg")
        E("P", lambda e: e.dma_start(out=outx[0:128, 0:2048], in_=z[:, 0:2048]),
          reads=[r_z], writes=[r_outT[0]], dma=True, desc="dz0")
        E("P", lambda e: e.dma_start(out=outx[128:256, 0:2048], in_=z[:, 2048:4096]),
          reads=[r_z], writes=[r_outT[1]], dma=True, desc="dz1")
        E("P", lambda e: e.dma_start(out=outx[256:384, 0:2048], in_=z[:, 4096:6144]),
          reads=[r_z], writes=[r_outT[2]], dma=True, desc="dz2")
        E("P", lambda e: e.dma_start(out=outx[384:512, 0:2048], in_=z[:, 6144:8192]),
          reads=[r_z], writes=[r_outT[3]], dma=True, desc="dz3")
        E("Y", lambda e: e.dma_start(out=outx[512:640, 0:8], in_=stats[:, 0:8]),
          reads=r_stats2, writes=[r_outT[4]], dma=True, desc="dstats")
        sch.force_inc_last()
        with es:
            with nc.Block() as block:
                sch.finalize(nc, block)
        return nc

    if "ffn1" in phases:
        dst = (outx, r_outT) if last_phase == "ffn1" else (x2d, r_x2dT)
        emit_ffn(xin, r_xinT, dst[0], dst[1], "f1")
    if "qkv" in phases:
        emit_qkv()
    if "gather" in phases:
        for b in range(NTB):
            emit_gather(b)
    if "attn" in phases:
        emit_attn()
    if "ffn2" in phases:
        emit_ffn(x3d, r_x3dT, outx, r_outT, "f2")

    if last_phase in ("qkv", "gather", "attn"):
        for gt in range(TOK // 128):
            sl = gt % 2
            E("Y", lambda e, sl=sl, gt=gt: e.dma_start(
                out=xt[sl][:, :], in_=x2d[gt * 128:(gt + 1) * 128, :]),
                reads=[r_x2dT[gt]], writes=[r_xt[sl]], dma=True, desc=f"dbgl{gt}")
            E("Y", lambda e, sl=sl, gt=gt: e.dma_start(
                out=outx[gt * 128:(gt + 1) * 128, :], in_=xt[sl][:, :]),
                reads=[r_xt[sl]], writes=[r_outT[gt]], dma=True, desc=f"dbgs{gt}")
    if last_phase == "attnres":
        for gt in range(TOK // 128):
            sl = gt % 2
            E("Y", lambda e, sl=sl, gt=gt: e.dma_start(
                out=xt[sl][:, :], in_=x3d[gt * 128:(gt + 1) * 128, :]),
                reads=[r_x3dT[gt]], writes=[r_xt[sl]], dma=True, desc=f"dbgl{gt}")
            E("Y", lambda e, sl=sl, gt=gt: e.dma_start(
                out=outx[gt * 128:(gt + 1) * 128, :], in_=xt[sl][:, :]),
                reads=[r_xt[sl]], writes=[r_outT[gt]], dma=True, desc=f"dbgs{gt}")

    sch.force_inc_last()
    with es:
        with nc.Block() as block:
            sch.finalize(nc, block)
    return nc



def prep_inputs(L0_attn_x, L0_attn_Wqkv, L0_attn_bqkv, L0_attn_Wout, L0_attn_bout,
                L0_ffn_w_gate, L0_ffn_w_up, L0_ffn_w_down):
    x = np.asarray(L0_attn_x, dtype=np.float32)
    Wqkv = np.asarray(L0_attn_Wqkv, dtype=np.float32)
    bqkv = np.asarray(L0_attn_bqkv, dtype=np.float32)
    Wout = np.asarray(L0_attn_Wout, dtype=np.float32)
    bout = np.asarray(L0_attn_bout, dtype=np.float32)
    wg = np.asarray(L0_ffn_w_gate, dtype=np.float32)
    wu = np.asarray(L0_ffn_w_up, dtype=np.float32)
    wd = np.asarray(L0_ffn_w_down, dtype=np.float32)

    g_pan = _pt_lhsT(_perm_cols(wg))
    u_pan = _pt_lhsT(_perm_cols(wu))
    wgu = np.ascontiguousarray(np.concatenate([g_pan, u_pan], axis=2))
    wdn = _pt_lhsT(_perm_rows(wd))
    wqk = _pt_lhsT(Wqkv[:, 0:2560])
    wv = np.ascontiguousarray(Wqkv[:, 2560:3072].reshape(NDC, 128, KVW))
    wo = _pt_lhsT(Wout)
    bqk = np.ascontiguousarray(bqkv[0:2560].reshape(20, 128).T)
    bv = np.ascontiguousarray(bqkv[2560:3072].reshape(1, KVW))
    bo = np.ascontiguousarray(bout.reshape(16, 128).T)
    shared = dict(wgu=wgu, wdn=wdn, wqk=wqk, wv=wv, wo=wo, bqk=bqk, bv=bv, bo=bo,
                  ones_in=np.ones((128, 128), np.float32),
                  id_in=np.eye(128, dtype=np.float32))
    in_maps = []
    for c in range(8):
        b, hh = c // 2, c % 2
        xin = np.ascontiguousarray(x[b, hh * TOK:(hh + 1) * TOK, :])
        in_maps.append(dict(shared, xin=xin))
    return in_maps


def kernel(L0_attn_x, L0_attn_Wqkv, L0_attn_bqkv, L0_attn_Wout, L0_attn_bout,
           L0_ffn_w_gate, L0_ffn_w_up, L0_ffn_w_down):
    in_maps = prep_inputs(L0_attn_x, L0_attn_Wqkv, L0_attn_bqkv, L0_attn_Wout,
                          L0_attn_bout, L0_ffn_w_gate, L0_ffn_w_up, L0_ffn_w_down)
    nc = build_program("ffn2")
    res = run_bass_kernel_spmd(nc, in_maps, core_ids=list(range(8)))
    out = np.empty((4, SFULL, D), dtype=np.float32)
    for c in range(8):
        b, hh = c // 2, c % 2
        out[b, hh * TOK:(hh + 1) * TOK, :] = res.results[c]["outx"]
    return out


# version 11
# speedup vs baseline: 1.0965x; 1.0617x over previous
import numpy as np
import ml_dtypes
from contextlib import ExitStack

import concourse.bass as bass
import concourse.mybir as mybir
from concourse.bass_utils import run_bass_kernel_spmd

F32 = mybir.dt.float32
BF16 = mybir.dt.bfloat16
AF = mybir.ActivationFunctionType
ALU = mybir.AluOpType

D = 2048
TOK = 2048
SFULL = 4096
TB = 512
NTB = TOK // TB
NDC = D // 128
IFF = 8192
NIC = IFF // 128
NH = 16
NG = 4
HD = 128
KVW = 512
EPS = 1e-5


class _Res:
    __slots__ = ("name", "writers", "readers")
    def __init__(self, name):
        self.name = name
        self.writers = []
        self.readers = []

class _Op:
    __slots__ = ("eng", "fn", "deps", "dma", "cc", "seq", "inc", "cum", "desc",
                 "sem_key", "sem_val", "issue_wait")
    def __init__(self, eng, fn, dma, cc, desc):
        self.eng = eng
        self.fn = fn
        self.deps = []
        self.dma = dma
        self.cc = cc
        self.seq = -1
        self.inc = dma or cc
        self.cum = 0
        self.desc = desc
        self.sem_key = None
        self.sem_val = 0
        self.issue_wait = None

class Sched:
    ENGS = ("T", "S", "V", "P", "Y")

    def __init__(self):
        self.ops = {e: [] for e in self.ENGS}
        self.order = []

    def res(self, name):
        return _Res(name)

    def emit(self, eng, fn, reads=(), writes=(), dma=False, cc=False, desc=""):
        op = _Op(eng, fn, dma, cc, desc)
        op.seq = len(self.ops[eng])
        deps = {}

        def add_dep(d):
            if d is None or d is op:
                return
            if d.eng == eng == "T" and not (d.dma or d.cc or op.dma or op.cc):
                return
            if d.dma or d.cc:
                deps[id(d)] = d
            else:
                k = "e" + d.eng
                cur = deps.get(k)
                if cur is None or d.seq > cur.seq:
                    deps[k] = d

        def add_reader(lst, d):
            if d.dma or d.cc:
                lst.append(d)
            else:
                for i, x in enumerate(lst):
                    if not (x.dma or x.cc) and x.eng == d.eng:
                        if d.seq > x.seq:
                            lst[i] = d
                        return
                lst.append(d)

        for r in reads:
            for d in r.writers:
                add_dep(d)
            add_reader(r.readers, op)
        for w in writes:
            if w.readers or not (dma and all(x.dma for x in w.writers)):
                for d in w.writers:
                    add_dep(d)
                for rd in w.readers:
                    add_dep(rd)
                w.writers = [op]
                w.readers = []
            else:
                w.writers.append(op)
        op.deps = list(deps.values())
        self.ops[eng].append(op)
        self.order.append(op)
        return op

    NPOOL = 8

    def finalize(self, nc, block):
        for op in self.order:
            for d in op.deps:
                d.inc = True
        sem_names = ["T", "S", "V"]
        for e in ("Y", "P"):
            sem_names += [f"{e}{i}" for i in range(self.NPOOL)]
        sem_names += ["C0", "C1"]
        es = ExitStack()
        sems = {n: es.enter_context(nc.semaphore(f"sem{n}")) for n in sem_names}
        self._es = es

        for e in ("T", "S", "V"):
            c = 0
            for op in self.ops[e]:
                if op.inc:
                    c += 1
                op.cum = c
                op.sem_key = e
                op.sem_val = c
        for e in ("Y", "P"):
            ndma = 0
            ncc = 0
            uses = {}
            for op in self.ops[e]:
                if op.dma:
                    key = f"{e}{ndma % self.NPOOL}"
                    ndma += 1
                elif op.cc:
                    key = f"C{ncc % 2}"
                    ncc += 1
                else:
                    raise AssertionError(f"non-dma non-cc op on queue {e}: {op.desc}")
                u = uses.get(key, 0)
                uses[key] = u + 1
                step = 16 if op.dma else 1
                op.sem_key = key
                op.sem_val = (u + 1) * step
                if u > 0:
                    op.issue_wait = (key, u * step)

        def mk_stream(ename):
            def stream(eng):
                waited = {}
                def wait(key, v):
                    if waited.get(key, 0) < v:
                        eng.wait_ge(sems[key], v)
                        waited[key] = v
                for op in self.ops[ename]:
                    if op.issue_wait is not None:
                        wait(*op.issue_wait)
                    byk = {}
                    for d in op.deps:
                        byk[d.sem_key] = max(byk.get(d.sem_key, 0), d.sem_val)
                    for k, v in byk.items():
                        wait(k, v)
                    inst = op.fn(eng)
                    if op.inc:
                        inst.then_inc(sems[op.sem_key], 16 if op.dma else 1)
                if ename == "Y":
                    fin = {}
                    for op in self.ops["Y"]:
                        if op.dma:
                            fin[op.sem_key] = max(fin.get(op.sem_key, 0), op.sem_val)
                    for k, v in fin.items():
                        wait(k, v)
            return stream

        block.tensor(mk_stream("T"))
        block.scalar(mk_stream("S"))
        block.vector(mk_stream("V"))
        block.gpsimd(mk_stream("P"))
        block.sync(mk_stream("Y"))

    def force_inc_last(self):
        for e in self.ENGS:
            if self.ops[e]:
                self.ops[e][-1].inc = True



def _pt_lhsT(w):
    Dd, O = w.shape
    return np.ascontiguousarray(
        w.reshape(Dd // 128, 128, O // 128, 128).transpose(2, 1, 0, 3)
         .reshape(O // 128, 128, (Dd // 128) * 128))

def _perm_cols(w):
    Dd, I = w.shape
    return np.ascontiguousarray(w.reshape(Dd, I // 4, 4).transpose(0, 2, 1).reshape(Dd, I))

def _perm_rows(w):
    I, Dd = w.shape
    return np.ascontiguousarray(w.reshape(I // 4, 4, Dd).transpose(1, 0, 2).reshape(I, Dd))



def build_program(last_phase="ffn2"):
    nc = bass.Bass("TRN2", target_bir_lowering=False)
    sch = Sched()
    PH = ["ffn1", "qkv", "gather", "attn", "attnres", "ffn2"]
    phases = [] if last_phase == "ln0" else PH[: PH.index(last_phase) + 1]

    xin = nc.dram_tensor("xin", [TOK, D], F32, kind="ExternalInput").ap()
    wgu = nc.dram_tensor("wgu", [NIC, 128, 2 * D], F32, kind="ExternalInput").ap()
    wdn = nc.dram_tensor("wdn", [NDC, 128, IFF], F32, kind="ExternalInput").ap()
    wqk = nc.dram_tensor("wqk", [20, 128, D], F32, kind="ExternalInput").ap()
    wv = nc.dram_tensor("wv", [NDC, 128, KVW], F32, kind="ExternalInput").ap()
    wo = nc.dram_tensor("wo", [NDC, 128, D], F32, kind="ExternalInput").ap()
    bqk_in = nc.dram_tensor("bqk", [128, 20], F32, kind="ExternalInput").ap()
    bv_in = nc.dram_tensor("bv", [1, KVW], F32, kind="ExternalInput").ap()
    bo_in = nc.dram_tensor("bo", [128, 16], F32, kind="ExternalInput").ap()
    ones_in = nc.dram_tensor("ones_in", [128, 128], F32, kind="ExternalInput").ap()
    id_in = nc.dram_tensor("id_in", [128, 128], F32, kind="ExternalInput").ap()
    outx = nc.dram_tensor("outx", [TOK, D], F32, kind="ExternalOutput").ap()

    x2d = nc.dram_tensor("x2d", [TOK, D], F32).ap()
    x3d = nc.dram_tensor("x3d", [TOK, D], F32).ap()
    aod = nc.dram_tensor("aod", [D, TOK], BF16).ap()
    wguc = nc.dram_tensor("wguc", [NIC, 128, 2 * D], BF16).ap()
    wdnc = nc.dram_tensor("wdnc", [NDC, 128, IFF], BF16).ap()
    wqkc = nc.dram_tensor("wqkc", [20, 128, D], BF16).ap()
    woc = nc.dram_tensor("woc", [NDC, 128, D], BF16).ap()
    ksend_t = nc.dram_tensor("ksend", [NTB, KVW, TB], BF16)
    krecv_t = nc.dram_tensor("krecv", [NTB, 2 * KVW, TB], BF16)
    vsend_t = nc.dram_tensor("vsend", [NTB, TB, KVW], BF16)
    vrecv_t = nc.dram_tensor("vrecv", [NTB, 2 * TB, KVW], BF16)
    ksend, krecv = ksend_t.ap(), krecv_t.ap()
    vsend, vrecv = vsend_t.ap(), vrecv_t.ap()

    es = ExitStack()
    sb = lambda name, shape, dt: es.enter_context(nc.sbuf_tensor(name, shape, dt))
    ps = lambda name, shape, dt: es.enter_context(nc.psum_tensor(name, shape, dt))

    xt = [sb(f"xt{i}", [128, D], F32) for i in range(2)]
    z = sb("z", [128, NDC * TB], BF16)
    hsq = sb("hsq", [128, 64 * 512], BF16)
    warn = sb("warn", [128, 64 * 512 // 2 * 2], BF16)
    hq = sb("hq", [128, 4 * TB], F32)
    habs = sb("habs", [128, 4 * TB], F32)
    norm = sb("norm", [128, D], BF16)
    woutb = [sb(f"woutb{i}", [128, D], BF16) for i in range(2)]
    aout = sb("aout", [128, TB], BF16)
    pt = sb("ptile", [128, 8 * 512], BF16)
    rcp = sb("rcp", [128, 512], F32)
    spt = sb("spt", [128, 3 * TB], F32)
    stats = sb("stats", [128, 16], F32)
    bqk_sb = sb("bqk_sb", [128, 20], F32)
    bo_sb = sb("bo_sb", [128, 16], F32)
    bv_sb = sb("bv_sb", [1, KVW], BF16)
    ones = sb("ones", [128, 128], BF16)
    ident = sb("ident", [128, 128], BF16)

    pApair = ps("pA", [128, 1024], F32)
    pBpair = ps("pB", [128, 1024], F32)
    pC = ps("pC", [128, 512], F32)
    pD = ps("pD", [128, 512], F32)
    pf = [pC[:, :], pD[:, :], pApair[:, 0:512], pBpair[:, 0:512],
          pApair[:, 512:1024], pBpair[:, 512:1024]]
    pb = [ps("pb0", [128, 1024], BF16)[:, :], ps("pb1", [128, 1024], BF16)[:, :]]

    R = sch.res
    r_xt = [R(f"xt{i}") for i in range(2)]
    r_z = R("z")
    r_hs = [R(f"hs{i}") for i in range(64)]
    r_wz = [R(f"wz{i}") for i in range(8)]
    r_hq = R("hq")
    r_habs = R("habs")
    r_norm = R("norm")
    r_woutb = [R(f"woutb{i}") for i in range(2)]
    r_aout = R("aout")
    r_pt = [R(f"pt{i}") for i in range(3)]
    r_ptq = [R(f"ptq{i}") for i in range(2)]
    r_rcp = R("rcp")
    r_spt = R("spt")
    r_stats2 = [R("statsA"), R("statsB")]
    r_pf = [R(f"pf{i}") for i in range(6)]
    r_pb = [R(f"pb{i}") for i in range(2)]
    r_x2dT = [R(f"x2d{i}") for i in range(16)]
    r_x3dT = [R(f"x3d{i}") for i in range(16)]
    r_outT = [R(f"out{i}") for i in range(16)]
    r_xinT = [R("xin")] * 16
    r_aod = R("aod")
    r_ksend = [R(f"ksend{i}") for i in range(NTB)]
    r_krecv = [R(f"krecv{i}") for i in range(NTB)]
    r_vsend = [R(f"vsend{i}") for i in range(NTB)]
    r_vrecv = [R(f"vrecv{i}") for i in range(NTB)]
    r_const = R("const")
    r_wguc = R("wguc")
    r_wdnc = R("wdnc")
    r_wqkc = R("wqkc")
    r_woc = R("woc")

    E = sch.emit

    E("Y", lambda e: e.dma_start(out=bqk_sb[:, :], in_=bqk_in), writes=[r_const], dma=True, desc="bqk")
    E("Y", lambda e: e.dma_start(out=bo_sb[:, :], in_=bo_in), writes=[r_const], dma=True, desc="bo")
    E("P", lambda e: e.dma_start(out=bv_sb[:, :], in_=bv_in), writes=[r_const], dma=True, desc="bv")
    E("P", lambda e: e.dma_start(out=ones[:, :], in_=ones_in), writes=[r_const], dma=True, desc="ones")
    E("P", lambda e: e.dma_start(out=ident[:, :], in_=id_in), writes=[r_const], dma=True, desc="ident")

    z3 = z[:, :].rearrange("p (c n) -> p c n", n=TB)
    hs3 = hsq[:, :].rearrange("p (c n) -> p c n", n=512)
    w_all = warn[:, :]

    def wzone(i0, n, shape_cols):
        apv = warn[:, i0 * 4096:(i0 + n) * 4096]
        return apv

    def emit_ln(src_ap, r_srcT, b, lnname):
        for t in range(4):
            sl = t % 2
            gt = b * 4 + t
            E("Y", lambda e, sl=sl, gt=gt: e.dma_start(
                out=xt[sl][:, :], in_=src_ap[gt * 128:(gt + 1) * 128, :]),
                reads=[r_srcT[gt]], writes=[r_xt[sl]], dma=True, desc=f"{lnname} ldx{gt}")
            c0 = (t % 2) * 4
            r_stats = r_stats2[t % 2]
            E("S", lambda e, sl=sl, c0=c0: e.activation(
                norm[:, :], xt[sl][:, :], AF.Identity, accum_out=stats[:, c0:c0 + 1]),
                reads=[r_xt[sl]], writes=[r_norm, r_stats], desc=f"{lnname} sum{t}")
            E("S", lambda e, sl=sl, c0=c0: e.activation(
                norm[:, :], xt[sl][:, :], AF.Square, accum_out=stats[:, c0 + 1:c0 + 2]),
                reads=[r_xt[sl]], writes=[r_norm, r_stats], desc=f"{lnname} sq{t}")
            E("V", lambda e, c0=c0: e.tensor_scalar(
                stats[:, c0 + 2:c0 + 3], stats[:, c0:c0 + 1], -1.0 / D, None, ALU.mult),
                reads=[r_stats], writes=[r_stats], desc="negmean")
            E("V", lambda e, c0=c0: e.tensor_scalar(
                stats[:, c0 + 3:c0 + 4], stats[:, c0 + 1:c0 + 2], 1.0 / D, EPS, ALU.mult, ALU.add),
                reads=[r_stats], writes=[r_stats], desc="ex2e")
            E("V", lambda e, c0=c0: e.tensor_tensor(
                stats[:, c0:c0 + 1], stats[:, c0 + 2:c0 + 3], stats[:, c0 + 2:c0 + 3], ALU.mult),
                reads=[r_stats], writes=[r_stats], desc="msq")
            E("V", lambda e, c0=c0: e.tensor_tensor(
                stats[:, c0 + 1:c0 + 2], stats[:, c0 + 3:c0 + 4], stats[:, c0:c0 + 1], ALU.subtract),
                reads=[r_stats], writes=[r_stats], desc="vpe")
            E("S", lambda e, c0=c0: e.activation(
                stats[:, c0:c0 + 1], stats[:, c0 + 1:c0 + 2], AF.Ln),
                reads=[r_stats], writes=[r_stats], desc="ln")
            E("S", lambda e, c0=c0: e.activation(
                stats[:, c0 + 1:c0 + 2], stats[:, c0:c0 + 1], AF.Exp, scale=-0.5),
                reads=[r_stats], writes=[r_stats], desc="rstd")
            E("V", lambda e, sl=sl, c0=c0: e.tensor_scalar(
                norm[:, :], xt[sl][:, :], stats[:, c0 + 2:c0 + 3], stats[:, c0 + 1:c0 + 2],
                ALU.add, ALU.mult),
                reads=[r_xt[sl], r_stats], writes=[r_norm], desc=f"{lnname} nrm{t}")
            for h in range(2):
                pbi = 1
                for j in range(8):
                    dc = h * 8 + j
                    E("T", lambda e, pbi=pbi, j=j, dc=dc, last=(j == 7): e.transpose(
                        pb[pbi][:, j * 128:(j + 1) * 128], norm[:, dc * 128:(dc + 1) * 128],
                        ident[:, :]),
                        reads=[r_norm, r_const], writes=[r_pb[pbi]], desc=f"tr{dc}")
                E("V", lambda e, pbi=pbi, h=h, t=t: e.tensor_copy(
                    z3[:, h * 8:(h + 1) * 8, t * 128:(t + 1) * 128],
                    pb[pbi][:, :].rearrange("p (c n) -> p c n", n=128)),
                    reads=[r_pb[pbi]], writes=[r_z], desc=f"zcp{t}.{h}")

    def emit_ffn(src_ap, r_srcT, dst_ap, r_dstT, tag):
        def emit_gu_load(c, cold):
            wsl = c % 3
            if cold:
                E("P", lambda e, c=c, wsl=wsl: e.dma_start(
                    out=warn[:, wsl * 4096:(wsl + 1) * 4096], in_=wgu[c]),
                    writes=[r_wz[wsl]], dma=True, desc=f"{tag}wgu{c}")
                E("Y", lambda e, c=c, wsl=wsl: e.dma_start(
                    out=wguc[c], in_=warn[:, wsl * 4096:(wsl + 1) * 4096]),
                    reads=[r_wz[wsl]], writes=[r_wguc], dma=True, desc=f"cgu{c}")
            else:
                E("P", lambda e, c=c, wsl=wsl: e.dma_start(
                    out=warn[:, wsl * 4096:(wsl + 1) * 4096], in_=wguc[c]),
                    reads=[r_wguc], writes=[r_wz[wsl]], dma=True, desc=f"{tag}bgu{c}")

        chunk_of = [k * 16 + q for q in range(16) for k in range(4)]
        for b in range(NTB):
            E("P", lambda e, b=b: e.dma_start(
                out=dst_ap[b * TB:(b + 1) * TB, :], in_=src_ap[b * TB:(b + 1) * TB, :]),
                reads=[r_srcT[i] for i in range(b * 4, b * 4 + 4)],
                writes=[r_dstT[i] for i in range(b * 4, b * 4 + 4)],
                dma=True, desc=f"{tag}pre{b}")
            emit_ln(src_ap, r_srcT, b, f"{tag}ln{b}")
            cold = (tag == "f1" and b == 0)
            next_load = 0 if b == 0 else 3
            for q in range(16):
                for k in range(4):
                    c = k * 16 + q
                    i = q * 4 + k
                    while next_load <= min(i + 2, 63):
                        emit_gu_load(chunk_of[next_load], cold)
                        next_load += 1
                    wsl = c % 3
                    gb = [0, 1][(q * 4 + k) % 2]
                    ub = [2, 3][(q * 4 + k) % 2]
                    pg, pu = pf[gb], pf[ub]
                    rg, ru = r_pf[gb], r_pf[ub]
                    for dc in range(NDC):
                        E("T", lambda e, pg=pg, wsl=wsl, dc=dc, first=(dc == 0), last=(dc == 15):
                            e.matmul(pg[:, :], warn[:, wsl * 4096 + dc * 128: wsl * 4096 + (dc + 1) * 128],
                                     z3[:, dc, :], start=first, stop=last),
                            reads=[r_wz[wsl], r_z], writes=[rg], desc=f"g{c}.{dc}")
                    for dc in range(NDC):
                        E("T", lambda e, pu=pu, wsl=wsl, dc=dc, first=(dc == 0), last=(dc == 15):
                            e.matmul(pu[:, :], warn[:, wsl * 4096 + 2048 + dc * 128: wsl * 4096 + 2048 + (dc + 1) * 128],
                                     z3[:, dc, :], start=first, stop=last),
                            reads=[r_wz[wsl], r_z], writes=[ru], desc=f"u{c}.{dc}")
                    E("S", lambda e, pg=pg, k=k: e.activation(
                        hq[:, k * TB:(k + 1) * TB], pg[:, :], AF.Silu),
                        reads=[rg], writes=[r_hq], desc=f"silu{c}")
                    E("V", lambda e, pu=pu, k=k: e.tensor_tensor(
                        hq[:, k * TB:(k + 1) * TB], hq[:, k * TB:(k + 1) * TB], pu[:, :], ALU.mult),
                        reads=[ru, r_hq], writes=[r_hq], desc=f"hmul{c}")
                E("S", lambda e: e.activation(habs[:, :], hq[:, :], AF.Abs),
                    reads=[r_hq], writes=[r_habs], desc=f"abs{q}")
                A = lambda k: habs[:, k * TB:(k + 1) * TB]
                t0, t1, t2 = spt[:, 0:TB], spt[:, TB:2 * TB], spt[:, 2 * TB:3 * TB]
                seq = [
                    (t0, A(0), A(1), ALU.max), (t1, A(2), A(3), ALU.max),
                    (t0, t0, t1, ALU.min),
                    (t1, A(0), A(1), ALU.min), (t2, A(2), A(3), ALU.min),
                    (t1, t1, t2, ALU.max),
                    (t1, t0, t1, ALU.max),
                ]
                for (o, i0, i1, opk) in seq:
                    E("V", lambda e, o=o, i0=i0, i1=i1, opk=opk: e.tensor_tensor(o, i0, i1, opk),
                        reads=[r_habs, r_spt], writes=[r_spt], desc="m2")
                for k in range(4):
                    c = k * 16 + q
                    E("V", lambda e, k=k: e.tensor_tensor(
                        A(k), A(k), spt[:, TB:2 * TB], ALU.is_ge),
                        reads=[r_habs, r_spt], writes=[r_habs], desc=f"ge{c}")
                    E("V", lambda e, k=k, c=c: e.tensor_tensor(
                        hs3[:, c, :], hq[:, k * TB:(k + 1) * TB], A(k), ALU.mult),
                        reads=[r_habs, r_hq], writes=[r_hs[c]], desc=f"hs{c}")
            for oc in range(NDC):
                dsl = 3 + (oc % 2) * 2
                if cold:
                    E("P", lambda e, oc=oc, dsl=dsl: e.dma_start(
                        out=warn[:, dsl * 4096:(dsl + 2) * 4096], in_=wdn[oc]),
                        writes=[r_wz[dsl], r_wz[dsl + 1]], dma=True, desc=f"{tag}wdn{oc}")
                    E("Y", lambda e, oc=oc, dsl=dsl: e.dma_start(
                        out=wdnc[oc], in_=warn[:, dsl * 4096:(dsl + 2) * 4096]),
                        reads=[r_wz[dsl], r_wz[dsl + 1]], writes=[r_wdnc], dma=True,
                        desc=f"cdn{oc}")
                else:
                    E("P", lambda e, oc=oc, dsl=dsl: e.dma_start(
                        out=warn[:, dsl * 4096:(dsl + 2) * 4096], in_=wdnc[oc]),
                        reads=[r_wdnc], writes=[r_wz[dsl], r_wz[dsl + 1]], dma=True,
                        desc=f"{tag}bdn{oc}")
                pd = pf[[0, 1, 2][oc % 3]]
                rd = r_pf[[0, 1, 2][oc % 3]]
                ics = [q2 + 16 * k2 for q2 in range(16) for k2 in range(4)]
                for idx, ic in enumerate(ics):
                    E("T", lambda e, pd=pd, dsl=dsl, ic=ic, first=(idx == 0), last=(idx == 63):
                        e.matmul(pd[:, :], warn[:, dsl * 4096 + ic * 128: dsl * 4096 + (ic + 1) * 128],
                                 hs3[:, ic, :], start=first, stop=last),
                        reads=[r_wz[dsl], r_wz[dsl + 1], r_hs[ic]], writes=[rd], desc=f"d{oc}.{ic}")
                E("S", lambda e, pd=pd: e.mul(aout[:, :], pd[:, :], 0.5),
                    reads=[rd], writes=[r_aout], desc=f"dev{oc}")
                pbi = 0
                for tt in range(4):
                    E("T", lambda e, pbi=pbi, tt=tt: e.transpose(
                        pb[pbi][:, tt * 128:(tt + 1) * 128],
                        aout[:, tt * 128:(tt + 1) * 128], ident[:, :]),
                        reads=[r_aout, r_const], writes=[r_pb[pbi]], desc=f"dtr{oc}.{tt}")
                E("V", lambda e, pbi=pbi, oc=oc: e.tensor_copy(
                    pt[:, :].rearrange("p (t n) -> p t n", n=D)[:, 0:2, oc * 128:(oc + 1) * 128],
                    pb[pbi][:, 0:256].rearrange("p (t n) -> p t n", n=128)),
                    reads=[r_pb[pbi]], writes=[r_pt[0], r_pt[1], r_ptq[0], r_ptq[1]],
                    desc=f"dcpA{oc}")
                E("V", lambda e, pbi=pbi, oc=oc: e.tensor_copy(
                    z[:, :].rearrange("p (t n) -> p t n", n=D)[:, 2:4, oc * 128:(oc + 1) * 128],
                    pb[pbi][:, 256:512].rearrange("p (t n) -> p t n", n=128)),
                    reads=[r_pb[pbi]], writes=[r_z], desc=f"dcpB{oc}")
            if b + 1 < NTB:
                ncold = (tag == "f1" and b + 1 == 0)
                for j in range(3):
                    emit_gu_load(chunk_of[j], ncold)
            ztok = z[:, :].rearrange("p (t n) -> p t n", n=D)
            pttok = pt[:, :].rearrange("p (t n) -> p t n", n=D)
            for tt in range(4):
                gt = b * 4 + tt
                srcv = pttok[:, tt, :] if tt < 2 else ztok[:, tt, :]
                srcr = ([r_pt[0], r_pt[1], r_ptq[0], r_ptq[1]] if tt < 2 else [r_z])
                E("P", lambda e, srcv=srcv, gt=gt: e.dma_start(
                    out=dst_ap[gt * 128:(gt + 1) * 128, :], in_=srcv,
                    accum_op=ALU.add),
                    reads=srcr + [r_dstT[gt]], writes=[r_dstT[gt]], dma=True,
                    desc=f"{tag}acc{gt}")

    def emit_qkv():
        for dc in range(NDC):
            E("P", lambda e, dc=dc: e.dma_start(
                out=warn[:, dc * 512:(dc + 1) * 512], in_=wv[dc]),
                writes=[r_wz[dc // 8]], dma=True, desc=f"wv{dc}")
        wv3 = warn[:, 0:NDC * 512].rearrange("p (c n) -> p c n", n=512)
        for b in range(NTB):
            emit_ln(x2d, r_x2dT, b, f"qln{b}")
            for oc in range(20):
                wsl = 4 + (oc % 2)
                if b == 0:
                    E("P", lambda e, oc=oc, wsl=wsl: e.dma_start(
                        out=warn[:, wsl * 4096: wsl * 4096 + 2048], in_=wqk[oc]),
                        writes=[r_wz[wsl]], dma=True, desc=f"wqk{oc}")
                    E("Y", lambda e, oc=oc, wsl=wsl: e.dma_start(
                        out=wqkc[oc], in_=warn[:, wsl * 4096: wsl * 4096 + 2048]),
                        reads=[r_wz[wsl]], writes=[r_wqkc], dma=True, desc=f"cqk{oc}")
                else:
                    E("P", lambda e, oc=oc, wsl=wsl: e.dma_start(
                        out=warn[:, wsl * 4096: wsl * 4096 + 2048], in_=wqkc[oc]),
                        reads=[r_wqkc], writes=[r_wz[wsl]], dma=True, desc=f"bqk{oc}")
                pq = pf[oc % 4]
                rq = r_pf[oc % 4]
                for dc in range(NDC):
                    E("T", lambda e, pq=pq, wsl=wsl, dc=dc, first=(dc == 0), last=(dc == 15):
                        e.matmul(pq[:, :], warn[:, wsl * 4096 + dc * 128: wsl * 4096 + (dc + 1) * 128],
                                 z3[:, dc, :], start=first, stop=last),
                        reads=[r_wz[wsl], r_z], writes=[rq], desc=f"qk{oc}.{dc}"),
                if oc < 16:
                    E("S", lambda e, pq=pq, oc=oc, b=b: e.activation(
                        hsq[:, oc * 2048 + b * TB: oc * 2048 + (b + 1) * TB], pq[:, :],
                        AF.Identity, bias=bqk_sb[:, oc:oc + 1]),
                        reads=[rq, r_const], writes=[r_hs[oc]], desc=f"qev{oc}"),
                else:
                    g = oc - 16
                    psl = (oc + b) % 2
                    E("S", lambda e, pq=pq, oc=oc, psl=psl: e.activation(
                        pt[:, 3072 + psl * 512: 3072 + (psl + 1) * 512], pq[:, :],
                        AF.Identity, bias=bqk_sb[:, oc:oc + 1]),
                        reads=[rq, r_const], writes=[r_ptq[psl]], desc=f"kev{oc}"),
                    E("Y", lambda e, g=g, b=b, psl=psl: e.dma_start(
                        out=ksend[b, g * 128:(g + 1) * 128, :],
                        in_=pt[:, 3072 + psl * 512: 3072 + (psl + 1) * 512]),
                        reads=[r_ptq[psl]], writes=[r_ksend[b]], dma=True, desc=f"kst{g}.{b}")
            for tt in range(4):
                pv = pf[4 + tt % 2]
                rv = r_pf[4 + tt % 2]
                for dc in range(NDC):
                    E("T", lambda e, pv=pv, dc=dc, tt=tt, first=(dc == 0):
                        e.matmul(pv[:, :], z3[:, dc, tt * 128:(tt + 1) * 128],
                                 wv3[:, dc, :], start=first, stop=False),
                        reads=[r_wz[0], r_wz[1], r_z], writes=[rv], desc=f"v{tt}.{dc}")
                E("T", lambda e, pv=pv: e.matmul(
                    pv[:, :], ones[0:1, :], bv_sb[:, :], start=False, stop=True),
                    reads=[r_const], writes=[rv], desc=f"vb{tt}")
                psl = tt % 2
                E("S", lambda e, pv=pv, psl=psl: e.activation(
                    pt[:, 3072 + psl * 512: 3072 + (psl + 1) * 512], pv[:, :], AF.Identity),
                    reads=[rv], writes=[r_ptq[psl]], desc=f"vev{tt}")
                E("Y", lambda e, b=b, tt=tt, psl=psl: e.dma_start(
                    out=vsend[b, tt * 128:(tt + 1) * 128, :],
                    in_=pt[:, 3072 + psl * 512: 3072 + (psl + 1) * 512]),
                    reads=[r_ptq[psl]], writes=[r_vsend[b]], dma=True, desc=f"vst{b}.{tt}")
            emit_gather_block(b)

    GROUPS = [[0, 1], [2, 3], [4, 5], [6, 7]]

    def emit_gather_block(b):
        E("P", lambda e, b=b: e.collective_compute(
            "AllGather", ALU.bypass, replica_groups=GROUPS,
            ins=[ksend_t.ap()[b].opt()], outs=[krecv_t.ap()[b].opt()]),
            reads=[r_ksend[b]], writes=[r_krecv[b]], cc=True, desc=f"ccK{b}")
        E("P", lambda e, b=b: e.collective_compute(
            "AllGather", ALU.bypass, replica_groups=GROUPS,
            ins=[vsend_t.ap()[b].opt()], outs=[vrecv_t.ap()[b].opt()]),
            reads=[r_vsend[b]], writes=[r_vrecv[b]], cc=True, desc=f"ccV{b}")

    def emit_gather(b):
        for g in range(NG):
            for r in range(2):
                E("Y", lambda e, g=g, r=r, b=b: e.dma_start(
                    out=warn[:, g * 4096 + r * 2048 + b * TB: g * 4096 + r * 2048 + (b + 1) * TB],
                    in_=krecv[b, r * KVW + g * 128: r * KVW + (g + 1) * 128, :]),
                    reads=[r_krecv[b]], writes=[r_wz[g]], dma=True, desc=f"kld{g}.{r}.{b}")
        for r in range(2):
            base = 4 * 4096 + (r * 16 + b * 4) * 512
            E("Y", lambda e, r=r, b=b, base=base: e.dma_start(
                out=warn[:, base: base + 4 * 512].rearrange("p (t n) -> p t n", n=512),
                in_=vrecv[b, r * TB:(r + 1) * TB, :].rearrange("(t p) n -> p t n", p=128)),
                reads=[r_vrecv[b]], writes=[r_wz[4 + (r * 16 + b * 4) // 8]],
                dma=True, desc=f"vld{r}.{b}")

    def emit_attn():
        k3 = warn[:, 0:4 * 4096].rearrange("p (g n) -> p g n", n=4096)
        v3 = warn[:, 4 * 4096:8 * 4096].rearrange("p (t n) -> p t n", n=512)
        q3 = hsq[:, :].rearrange("p (h n) -> p h n", n=2048)
        NKT = SFULL // 128
        for qb in range(NTB):
            for h in range(NH):
                g = h % NG
                po = pf[0]
                pr = pf[1]
                ro, rr = r_pf[0], r_pf[1]
                ST_BANKS = [2, 4, 3, 5]
                def st_op(kt):
                    bank = ST_BANKS[kt % 4]
                    E("T", lambda e, kt=kt, bank=bank, g=g, h=h, qb=qb: e.matmul(
                        pf[bank][:, :], k3[:, g, kt * 128:(kt + 1) * 128],
                        q3[:, h, qb * TB:(qb + 1) * TB], start=True, stop=True),
                        reads=[r_wz[g], r_hs[h]], writes=[r_pf[bank]], desc=f"st{h}.{kt}")
                for kt in range(4):
                    st_op(kt)
                for kp in range(NKT // 2):
                    k0 = 2 * kp
                    pair_t = pApair if (k0 % 4) == 0 else pBpair
                    prs = [r_pf[2], r_pf[4]] if (k0 % 4) == 0 else [r_pf[3], r_pf[5]]
                    psl = kp % 3
                    E("S", lambda e, pair_t=pair_t, psl=psl: e.activation(
                        pt[:, psl * 1024:(psl + 1) * 1024],
                        pair_t[:, :], AF.Exp, scale=1.0 / HD),
                        reads=prs, writes=[r_pt[psl]], desc=f"exp{h}.{kp}")
                    if k0 + 5 < NKT:
                        st_op(k0 + 4)
                        st_op(k0 + 5)
                    for j in range(2):
                        kt = k0 + j
                        pslc = pt[:, psl * 1024 + j * 512: psl * 1024 + (j + 1) * 512]
                        E("T", lambda e, kt=kt, g=g, pslc=pslc, first=(kt == 0), last=(kt == NKT - 1):
                            e.matmul(po[:, :], v3[:, kt, g * 128:(g + 1) * 128],
                                     pslc, start=first, stop=last),
                            reads=[r_wz[4 + kt // 8], r_pt[psl]], writes=[ro], desc=f"pv{h}.{kt}")
                        E("T", lambda e, kt=kt, pslc=pslc, first=(kt == 0), last=(kt == NKT - 1):
                            e.matmul(pr[:, :], ones[:, :], pslc, start=first, stop=last),
                            reads=[r_const, r_pt[psl]], writes=[rr], desc=f"rs{h}.{kt}")
                E("V", lambda e, pr=pr: e.reciprocal(rcp[:, :], pr[:, :]),
                    reads=[rr], writes=[r_rcp], desc=f"rcp{h}")
                E("V", lambda e, po=po, h=h: e.tensor_tensor(
                    z3[:, h, :], po[:, :], rcp[:, :], ALU.mult),
                    reads=[ro, r_rcp], writes=[r_z], desc=f"at{h}")
            for oc in range(NDC):
                wsl = oc % 2
                if qb == 0:
                    E("P", lambda e, oc=oc, wsl=wsl: e.dma_start(
                        out=woutb[wsl][:, :], in_=wo[oc]),
                        writes=[r_woutb[wsl]], dma=True, desc=f"wo{oc}")
                    E("Y", lambda e, oc=oc, wsl=wsl: e.dma_start(
                        out=woc[oc], in_=woutb[wsl][:, :]),
                        reads=[r_woutb[wsl]], writes=[r_woc], dma=True, desc=f"cwo{oc}")
                else:
                    E("P", lambda e, oc=oc, wsl=wsl: e.dma_start(
                        out=woutb[wsl][:, :], in_=woc[oc]),
                        reads=[r_woc], writes=[r_woutb[wsl]], dma=True, desc=f"bwo{oc}")
                pp = pf[oc % 2]
                rp = r_pf[oc % 2]
                for dc in range(NDC):
                    E("T", lambda e, pp=pp, wsl=wsl, dc=dc, first=(dc == 0), last=(dc == 15):
                        e.matmul(pp[:, :], woutb[wsl][:, dc * 128:(dc + 1) * 128],
                                 z3[:, dc, :], start=first, stop=last),
                        reads=[r_woutb[wsl], r_z], writes=[rp], desc=f"op{oc}.{dc}")
                E("S", lambda e, pp=pp, oc=oc: e.activation(
                    aout[:, :], pp[:, :], AF.Identity, bias=bo_sb[:, oc:oc + 1]),
                    reads=[rp, r_const], writes=[r_aout], desc=f"oev{oc}")
                E("Y", lambda e, oc=oc, qb=qb: e.dma_start(
                    out=aod[oc * 128:(oc + 1) * 128, qb * TB:(qb + 1) * TB],
                    in_=aout[:, :]),
                    reads=[r_aout], writes=[r_aod], dma=True, desc=f"ost{oc}.{qb}")
            emit_attnres(qb)

    def emit_attnres(qb):
        stg = [z[:, 0:D], z[:, D:2 * D], z[:, 2 * D:3 * D]]
        rstg = [r_z, r_z, r_z]
        xst = [hq, habs]
        rxst = [r_hq, r_habs]
        for tt in range(4):
            gt = qb * 4 + tt
            ssl = gt % 3
            sl = gt % 2
            E("Y", lambda e, ssl=ssl, gt=gt: e.dma_start_transpose(
                stg[ssl][:, :], aod[:, gt * 128:(gt + 1) * 128]),
                reads=[r_aod], writes=[rstg[ssl]], dma=True, desc=f"atr{gt}")
            E("Y", lambda e, sl=sl, gt=gt: e.dma_start(
                out=xst[sl][:, 0:D], in_=x2d[gt * 128:(gt + 1) * 128, :]),
                reads=[r_x2dT[gt]], writes=[rxst[sl]], dma=True, desc=f"arx{gt}")
            E("V", lambda e, sl=sl, ssl=ssl: e.tensor_tensor(
                xst[sl][:, 0:D], xst[sl][:, 0:D], stg[ssl][:, :], ALU.add),
                reads=[rstg[ssl], rxst[sl]], writes=[rxst[sl]], desc=f"ares{gt}")
            E("Y", lambda e, sl=sl, gt=gt: e.dma_start(
                out=x3d[gt * 128:(gt + 1) * 128, :], in_=xst[sl][:, 0:D]),
                reads=[rxst[sl]], writes=[r_x3dT[gt]], dma=True, desc=f"asx{gt}")

    if last_phase == "ln0":
        emit_ln(xin, R("xin"), 0, "dbg")
        E("P", lambda e: e.dma_start(out=outx[0:128, 0:2048], in_=z[:, 0:2048]),
          reads=[r_z], writes=[r_outT[0]], dma=True, desc="dz0")
        E("P", lambda e: e.dma_start(out=outx[128:256, 0:2048], in_=z[:, 2048:4096]),
          reads=[r_z], writes=[r_outT[1]], dma=True, desc="dz1")
        E("P", lambda e: e.dma_start(out=outx[256:384, 0:2048], in_=z[:, 4096:6144]),
          reads=[r_z], writes=[r_outT[2]], dma=True, desc="dz2")
        E("P", lambda e: e.dma_start(out=outx[384:512, 0:2048], in_=z[:, 6144:8192]),
          reads=[r_z], writes=[r_outT[3]], dma=True, desc="dz3")
        E("Y", lambda e: e.dma_start(out=outx[512:640, 0:8], in_=stats[:, 0:8]),
          reads=r_stats2, writes=[r_outT[4]], dma=True, desc="dstats")
        sch.force_inc_last()
        with es:
            with nc.Block() as block:
                sch.finalize(nc, block)
        return nc

    if "ffn1" in phases:
        dst = (outx, r_outT) if last_phase == "ffn1" else (x2d, r_x2dT)
        emit_ffn(xin, r_xinT, dst[0], dst[1], "f1")
    if "qkv" in phases:
        emit_qkv()
    if "gather" in phases:
        for b in range(NTB):
            emit_gather(b)
    if "attn" in phases:
        emit_attn()
    if "ffn2" in phases:
        emit_ffn(x3d, r_x3dT, outx, r_outT, "f2")

    if last_phase in ("qkv", "gather", "attn"):
        for gt in range(TOK // 128):
            sl = gt % 2
            E("Y", lambda e, sl=sl, gt=gt: e.dma_start(
                out=xt[sl][:, :], in_=x2d[gt * 128:(gt + 1) * 128, :]),
                reads=[r_x2dT[gt]], writes=[r_xt[sl]], dma=True, desc=f"dbgl{gt}")
            E("Y", lambda e, sl=sl, gt=gt: e.dma_start(
                out=outx[gt * 128:(gt + 1) * 128, :], in_=xt[sl][:, :]),
                reads=[r_xt[sl]], writes=[r_outT[gt]], dma=True, desc=f"dbgs{gt}")
    if last_phase == "attnres":
        for gt in range(TOK // 128):
            sl = gt % 2
            E("Y", lambda e, sl=sl, gt=gt: e.dma_start(
                out=xt[sl][:, :], in_=x3d[gt * 128:(gt + 1) * 128, :]),
                reads=[r_x3dT[gt]], writes=[r_xt[sl]], dma=True, desc=f"dbgl{gt}")
            E("Y", lambda e, sl=sl, gt=gt: e.dma_start(
                out=outx[gt * 128:(gt + 1) * 128, :], in_=xt[sl][:, :]),
                reads=[r_xt[sl]], writes=[r_outT[gt]], dma=True, desc=f"dbgs{gt}")

    sch.force_inc_last()
    with es:
        with nc.Block() as block:
            sch.finalize(nc, block)
    return nc



def prep_inputs(L0_attn_x, L0_attn_Wqkv, L0_attn_bqkv, L0_attn_Wout, L0_attn_bout,
                L0_ffn_w_gate, L0_ffn_w_up, L0_ffn_w_down):
    x = np.asarray(L0_attn_x, dtype=np.float32)
    Wqkv = np.asarray(L0_attn_Wqkv, dtype=np.float32)
    bqkv = np.asarray(L0_attn_bqkv, dtype=np.float32)
    Wout = np.asarray(L0_attn_Wout, dtype=np.float32)
    bout = np.asarray(L0_attn_bout, dtype=np.float32)
    wg = np.asarray(L0_ffn_w_gate, dtype=np.float32)
    wu = np.asarray(L0_ffn_w_up, dtype=np.float32)
    wd = np.asarray(L0_ffn_w_down, dtype=np.float32)

    g_pan = _pt_lhsT(_perm_cols(wg))
    u_pan = _pt_lhsT(_perm_cols(wu))
    wgu = np.ascontiguousarray(np.concatenate([g_pan, u_pan], axis=2))
    wdn = _pt_lhsT(_perm_rows(wd))
    wqk = _pt_lhsT(Wqkv[:, 0:2560])
    wv = np.ascontiguousarray(Wqkv[:, 2560:3072].reshape(NDC, 128, KVW))
    wo = _pt_lhsT(Wout)
    bqk = np.ascontiguousarray(bqkv[0:2560].reshape(20, 128).T)
    bv = np.ascontiguousarray(bqkv[2560:3072].reshape(1, KVW))
    bo = np.ascontiguousarray(bout.reshape(16, 128).T)
    shared = dict(wgu=wgu, wdn=wdn, wqk=wqk, wv=wv, wo=wo, bqk=bqk, bv=bv, bo=bo,
                  ones_in=np.ones((128, 128), np.float32),
                  id_in=np.eye(128, dtype=np.float32))
    in_maps = []
    for c in range(8):
        b, hh = c // 2, c % 2
        xin = np.ascontiguousarray(x[b, hh * TOK:(hh + 1) * TOK, :])
        in_maps.append(dict(shared, xin=xin))
    return in_maps


def kernel(L0_attn_x, L0_attn_Wqkv, L0_attn_bqkv, L0_attn_Wout, L0_attn_bout,
           L0_ffn_w_gate, L0_ffn_w_up, L0_ffn_w_down):
    in_maps = prep_inputs(L0_attn_x, L0_attn_Wqkv, L0_attn_bqkv, L0_attn_Wout,
                          L0_attn_bout, L0_ffn_w_gate, L0_ffn_w_up, L0_ffn_w_down)
    nc = build_program("ffn2")
    res = run_bass_kernel_spmd(nc, in_maps, core_ids=list(range(8)))
    out = np.empty((4, SFULL, D), dtype=np.float32)
    for c in range(8):
        b, hh = c // 2, c % 2
        out[b, hh * TOK:(hh + 1) * TOK, :] = res.results[c]["outx"]
    return out
